# version 01
# baseline (speedup 1.0000x reference)
from contextlib import ExitStack
import numpy as np
import concourse.bass as bass
import concourse.mybir as mybir
from concourse.bass_utils import run_bass_kernel_spmd

F32 = mybir.dt.float32
AF = mybir.ActivationFunctionType
ALU = mybir.AluOpType
AX = mybir.AxisListType.X

D = 1024
NCORE = 8
SEQ = 8192
BATCH = 2
NTOK = BATCH * SEQ


class Prog:

    def __init__(self):
        self.ops = []
        self.c_tot = 0
        self.d_tot = 0

    def _emit(self, eng, method, *args, **kwargs):
        is_dma = method == "dma_start"
        self.ops.append((eng, method, args, kwargs, self.c_tot, self.d_tot, is_dma))
        if is_dma:
            self.d_tot += 16
        else:
            self.c_tot += 1

    def dma(self, dst, src, **kw):
        self._emit("sync", "dma_start", out=dst, in_=src, **kw)

    def mm(self, out, lhsT, rhs, start, stop):
        self._emit("tensor", "matmul", out, lhsT, rhs, start=start, stop=stop)

    def tp(self, out, in_, ident):
        self._emit("tensor", "transpose", out, in_, ident)

    def act(self, out, in_, func, bias=0.0, scale=1.0):
        self._emit("scalar", "activation", out, in_, func, bias=bias, scale=scale)

    def s_add(self, out, in_, c):
        self._emit("scalar", "add", out, in_, c)

    def s_mul(self, out, in_, c):
        self._emit("scalar", "mul", out, in_, c)

    def cp(self, out, in_):
        self._emit("scalar", "copy", out, in_)

    def tt(self, out, a, b, op):
        self._emit("vector", "tensor_tensor", out, a, b, op)

    def ts(self, out, in0, s1, s2, op0, op1=None):
        if s2 is None and op1 is None:
            self._emit("vector", "tensor_single_scalar", out, in0, s1, op0)
        else:
            self._emit("vector", "tensor_scalar", out, in0, s1, s2, op0, op1)

    def stt(self, out, in0, scalar, in1, op0, op1):
        self._emit("vector", "scalar_tensor_tensor", out, in0, scalar, in1, op0, op1)

    def rsum(self, out, in_):
        self._emit("vector", "reduce_sum", out, in_, AX)

    def rmax(self, out, in_):
        self._emit("vector", "reduce_max", out, in_, AX)

    def recip(self, out, in_):
        self._emit("vector", "reciprocal", out, in_)

    def memset(self, ap, v):
        self._emit("vector", "memset", ap, v)

    def build_block(self, nc):
        with (
            nc.semaphore("csem") as csem,
            nc.semaphore("dsem") as dsem,
            nc.Block() as block,
        ):
            def mk(eng_name):
                def run(eng):
                    for (en, meth, args, kwargs, cw, dw, is_dma) in self.ops:
                        if en != eng_name:
                            continue
                        if cw > 0:
                            eng.wait_ge(csem, cw)
                        if dw > 0:
                            eng.wait_ge(dsem, dw)
                        instr = getattr(eng, meth)(*args, **kwargs)
                        instr.then_inc(dsem if is_dma else csem, 16 if is_dma else 1)
                    if eng_name == "sync":
                        eng.wait_ge(csem, self.c_tot)
                        eng.wait_ge(dsem, self.d_tot)
                return run

            block.sync(mk("sync"))
            block.tensor(mk("tensor"))
            block.scalar(mk("scalar"))
            block.vector(mk("vector"))


def emit_ln(p, xin, xc, scr, ln, mrow, vrow, width=D):
    p.rsum(mrow[:, 0:1], xin[:, 0:width])
    p.ts(mrow[:, 1:2], mrow[:, 0:1], -1.0 / width, None, ALU.mult)
    p.ts(xc[:, 0:width], xin[:, 0:width], mrow[:, 1:2], None, ALU.add)
    p.act(scr[:, 0:width], xc[:, 0:width], AF.Square)
    p.rsum(vrow[:, 0:1], scr[:, 0:width])
    p.ts(vrow[:, 1:2], vrow[:, 0:1], 1.0 / width, 1e-5, ALU.mult, ALU.add)
    p.act(vrow[:, 2:3], vrow[:, 1:2], AF.Sqrt)
    p.recip(vrow[:, 3:4], vrow[:, 2:3])
    p.ts(ln[:, 0:width], xc[:, 0:width], vrow[:, 3:4], None, ALU.mult)


def emit_tp8(p, ln, lnT, ps_t, ident, nblk=8):
    for i in range(nblk):
        p.tp(ps_t[:, 0:128], ln[:, i * 128:(i + 1) * 128], ident)
        p.cp(lnT[:, i * 128:(i + 1) * 128], ps_t[:, 0:128])


def emit_router(p, psG, probs_scr, comb_out, mrow):
    ex = probs_scr[:, 0:8]
    pr = probs_scr[:, 8:16]
    eq = probs_scr[:, 16:24]
    pw = probs_scr[:, 24:32]
    mk2 = probs_scr[:, 32:40]
    p.rmax(mrow[:, 0:1], psG[:, 0:8])
    p.ts(mrow[:, 1:2], mrow[:, 0:1], -1.0, None, ALU.mult)
    p.act(ex, psG[:, 0:8], AF.Exp, bias=mrow[:, 1:2])
    p.rsum(mrow[:, 2:3], ex)
    p.recip(mrow[:, 3:4], mrow[:, 2:3])
    p.ts(pr, ex, mrow[:, 3:4], None, ALU.mult)
    p.rmax(mrow[:, 4:5], pr)
    p.ts(eq, pr, mrow[:, 4:5], None, ALU.is_ge)
    p.ts(pw, eq, -1.0, 1.0, ALU.mult, ALU.add)
    p.tt(pw, pr, pw, ALU.mult)
    p.rmax(mrow[:, 5:6], pw)
    p.ts(mk2, pr, mrow[:, 5:6], None, ALU.is_ge)
    p.tt(comb_out, pr, mk2, ALU.mult)


def build_A():
    nc = bass.Bass("TRN2", target_bir_lowering=False)
    p = Prog()
    xb = nc.dram_tensor("xb", [SEQ, D], F32, kind="ExternalInput").ap()
    wq = nc.dram_tensor("wq", [D, 256], F32, kind="ExternalInput").ap()
    wk = nc.dram_tensor("wk", [D, 256], F32, kind="ExternalInput").ap()
    wk2 = nc.dram_tensor("wk2", [D, 256], F32, kind="ExternalInput").ap()
    bq = nc.dram_tensor("bq", [256], F32, kind="ExternalInput").ap()
    bk = nc.dram_tensor("bk", [256], F32, kind="ExternalInput").ap()
    bk2 = nc.dram_tensor("bk2", [256], F32, kind="ExternalInput").ap()
    cns = nc.dram_tensor("cns", [128, 384], F32, kind="ExternalInput").ap()
    rws = nc.dram_tensor("rws", [1, 256], F32, kind="ExternalInput").ap()
    aout = nc.dram_tensor("aout", [SEQ, 256], F32, kind="ExternalOutput").ap()

    st = ExitStack()
    with st:
        EN = st.enter_context
        consts = EN(nc.sbuf_tensor("consts", [128, 384], F32))
        rows = EN(nc.sbuf_tensor("rows", [1, 256], F32))
        brow = EN(nc.sbuf_tensor("brow", [1, 768], F32))
        wqs = EN(nc.sbuf_tensor("wqs", [128, 2048], F32))
        wks = EN(nc.sbuf_tensor("wks", [128, 2048], F32))
        wk2s = EN(nc.sbuf_tensor("wk2s", [128, 2048], F32))
        xin = EN(nc.sbuf_tensor("xin", [128, D], F32))
        xc = EN(nc.sbuf_tensor("xc0", [128, D], F32))
        scr = EN(nc.sbuf_tensor("scr", [128, D], F32))
        ln = EN(nc.sbuf_tensor("ln", [128, D], F32))
        lnT = EN(nc.sbuf_tensor("lnT", [128, D], F32))
        mrow = EN(nc.sbuf_tensor("mrow", [128, 8], F32))
        Q = EN(nc.sbuf_tensor("Q", [128, 256], F32))
        K = EN(nc.sbuf_tensor("K", [128, 256], F32))
        T2 = EN(nc.sbuf_tensor("T2", [128, 256], F32))
        KV = EN(nc.sbuf_tensor("KV", [128, 256], F32))
        Kc = EN(nc.sbuf_tensor("Kc", [128, 256], F32))
        rKc = EN(nc.sbuf_tensor("rKc", [128, 256], F32))
        S = EN(nc.sbuf_tensor("S", [128, 256], F32))
        Oar = EN(nc.sbuf_tensor("Oar", [128, 256], F32))
        E = EN(nc.sbuf_tensor("E", [128, 256], F32))
        K2 = EN(nc.sbuf_tensor("K2", [128, 256], F32))
        K2E = EN(nc.sbuf_tensor("K2E", [128, 256], F32))
        S2 = EN(nc.sbuf_tensor("S2", [128, 256], F32))
        Oma = EN(nc.sbuf_tensor("Oma", [128, 256], F32))
        Ao = EN(nc.sbuf_tensor("Ao", [128, 256], F32))
        carK = EN(nc.sbuf_tensor("carK", [1, 256], F32))
        carKV = EN(nc.sbuf_tensor("carKV", [1, 256], F32))
        carS2 = EN(nc.sbuf_tensor("carS2", [1, 256], F32))
        pOar = EN(nc.sbuf_tensor("pOar", [1, 256], F32))
        pT2 = EN(nc.sbuf_tensor("pT2", [1, 256], F32))
        pQ = EN(nc.sbuf_tensor("pQ", [1, 256], F32))
        cext = EN(nc.sbuf_tensor("cext", [128, 256], F32))
        ps_t = EN(nc.psum_tensor("ps_t", [128, 128], F32))
        psX = EN(nc.psum_tensor("psX", [128, 256], F32))
        ident = consts[:, 0:128]
        cumA = consts[:, 128:256]
        shiftA = consts[:, 256:384]
        ones1 = rows[0:1, 0:128]
        sel0 = rows[0:1, 128:256]
        p.dma(consts[:], cns)
        p.dma(rows[:], rws)
        p.dma(brow[0:1, 0:256], bq[None, :])
        p.dma(brow[0:1, 256:512], bk[None, :])
        p.dma(brow[0:1, 512:768], bk2[None, :])
        for i in range(8):
            p.dma(wqs[:, i * 256:(i + 1) * 256], wq[i * 128:(i + 1) * 128, :])
            p.dma(wks[:, i * 256:(i + 1) * 256], wk[i * 128:(i + 1) * 128, :])
            p.dma(wk2s[:, i * 256:(i + 1) * 256], wk2[i * 128:(i + 1) * 128, :])
        for t in (carK, carKV, carS2, pOar, pQ):
            p.memset(t[0:1, :], 0.0)
        p.ts(cext[0:1, 0:256], brow[0:1, 512:768], 6.25e-4, None, ALU.mult)
        p.act(pT2[0:1, :], cext[0:1, 0:256], AF.Sigmoid)

        for c in range(SEQ // 128):
            p.dma(xin[:], xb[c * 128:(c + 1) * 128, :])
            emit_ln(p, xin, xc, scr, ln, mrow, mrow)
            emit_tp8(p, ln, lnT, ps_t, ident)
            V = ln[:, 0:256]
            for (ws, bo, dst, fn, sc) in (
                (wqs, 0, Q, AF.Sigmoid, None),
                (wks, 256, K, AF.Exp, None),
                (wk2s, 512, T2, AF.Sigmoid, 6.25e-4),
            ):
                for i in range(8):
                    p.mm(psX[:], lnT[:, i * 128:(i + 1) * 128],
                         ws[:, i * 256:(i + 1) * 256], start=(i == 0), stop=False)
                p.mm(psX[:], ones1, brow[0:1, bo:bo + 256], start=False, stop=True)
                if sc is None:
                    p.act(dst[:], psX[:], fn)
                else:
                    p.ts(scr[:, 0:256], psX[:], sc, None, ALU.mult)
                    p.act(dst[:], scr[:, 0:256], fn)
            p.tt(KV[:], K[:], V, ALU.mult)
            p.mm(psX[:], cumA, K[:], start=True, stop=False)
            p.mm(psX[:], ones1, carK[0:1, :], start=False, stop=True)
            p.cp(cext[:], psX[:])
            p.dma(carK[0:1, :], cext[127:128, :])
            p.ts(Kc[:], psX[:], 1e-6, None, ALU.add)
            p.recip(rKc[:], Kc[:])
            p.mm(psX[:], cumA, KV[:], start=True, stop=False)
            p.mm(psX[:], ones1, carKV[0:1, :], start=False, stop=True)
            p.cp(cext[:], psX[:])
            p.dma(carKV[0:1, :], cext[127:128, :])
            p.tt(S[:], psX[:], rKc[:], ALU.mult)
            p.tt(Oar[:], S[:], Q[:], ALU.mult)
            p.mm(psX[:], shiftA, Oar[:], start=True, stop=False)
            p.mm(psX[:], sel0, pOar[0:1, :], start=False, stop=True)
            p.tt(E[:], V, psX[:], ALU.subtract)
            if c == 0:
                p.memset(E[0:1, :], 0.0)
            p.dma(pOar[0:1, :], Oar[127:128, :])
            p.mm(psX[:], shiftA, T2[:], start=True, stop=False)
            p.mm(psX[:], sel0, pT2[0:1, :], start=False, stop=True)
            p.cp(K2[:], psX[:])
            p.dma(pT2[0:1, :], T2[127:128, :])
            p.tt(K2E[:], K2[:], E[:], ALU.mult)
            p.mm(psX[:], cumA, K2E[:], start=True, stop=False)
            p.mm(psX[:], ones1, carS2[0:1, :], start=False, stop=True)
            p.cp(S2[:], psX[:])
            p.dma(carS2[0:1, :], S2[127:128, :])
            p.mm(psX[:], shiftA, Q[:], start=True, stop=False)
            p.mm(psX[:], sel0, pQ[0:1, :], start=False, stop=True)
            p.stt(Oma[:], psX[:], 6.25e-4, S2[:], ALU.mult, ALU.mult)
            p.dma(pQ[0:1, :], Q[127:128, :])
            p.tt(Ao[:], Oar[:], Oma[:], ALU.add)
            p.dma(aout[c * 128:(c + 1) * 128, :], Ao[:])
        p.build_block(nc)
    return nc


def build_B():
    nc = bass.Bass("TRN2", target_bir_lowering=False)
    p = Prog()
    TOK = 2048
    QT = 512
    xsh = nc.dram_tensor("xsh", [TOK, D], F32, kind="ExternalInput").ap()
    ash = nc.dram_tensor("ash", [TOK, D], F32, kind="ExternalInput").ap()
    wprojT = nc.dram_tensor("wprojT", [D, D], F32, kind="ExternalInput").ap()
    bproj = nc.dram_tensor("bproj", [D], F32, kind="ExternalInput").ap()
    routw = nc.dram_tensor("routw", [D, 8], F32, kind="ExternalInput").ap()
    w1 = nc.dram_tensor("w1", [8, D, D], F32, kind="ExternalInput").ap()
    w2 = nc.dram_tensor("w2", [8, D, D], F32, kind="ExternalInput").ap()
    sgu = nc.dram_tensor("sgu", [D, 2048], F32, kind="ExternalInput").ap()
    sdn = nc.dram_tensor("sdn", [D, D], F32, kind="ExternalInput").ap()
    sgw = nc.dram_tensor("sgw", [D, 1], F32, kind="ExternalInput").ap()
    cns = nc.dram_tensor("cns", [128, 128], F32, kind="ExternalInput").ap()
    rws = nc.dram_tensor("rws", [1, 128], F32, kind="ExternalInput").ap()
    x1scr = nc.dram_tensor("x1scr", [TOK, D], F32, kind="Internal").ap()
    x2o = nc.dram_tensor("x2o", [TOK, D], F32, kind="ExternalOutput").ap()

    nch = QT // 128
    st = ExitStack()
    with st:
        EN = st.enter_context
        identt = EN(nc.sbuf_tensor("ident", [128, 128], F32))
        rows = EN(nc.sbuf_tensor("rows", [1, 128], F32))
        brow = EN(nc.sbuf_tensor("brow", [1, D], F32))
        big = EN(nc.sbuf_tensor("big", [128, 16384], F32))
        wdn = EN(nc.sbuf_tensor("wdn", [128, 8192], F32))
        rw = EN(nc.sbuf_tensor("rw", [128, 64], F32))
        sgws = EN(nc.sbuf_tensor("sgws", [128, 8], F32))
        lnTg = EN(nc.sbuf_tensor("lnTg", [128, 4096], F32))
        accg = EN(nc.sbuf_tensor("accg", [128, 4096], F32))
        s1 = EN(nc.sbuf_tensor("s1", [128, D], F32))
        s2 = EN(nc.sbuf_tensor("s2", [128, D], F32))
        s3 = EN(nc.sbuf_tensor("s3", [128, D], F32))
        s4 = EN(nc.sbuf_tensor("s4", [128, D], F32))
        s5 = EN(nc.sbuf_tensor("s5", [128, D], F32))
        s6 = EN(nc.sbuf_tensor("s6", [128, D], F32))
        mrow = EN(nc.sbuf_tensor("mrow", [128, 8], F32))
        pscr = EN(nc.sbuf_tensor("pscr", [128, 40], F32))
        comb = EN(nc.sbuf_tensor("comb", [128, 32], F32))
        ps_t = EN(nc.psum_tensor("ps_t", [128, 128], F32))
        psA = EN(nc.psum_tensor("psA", [128, 512], F32))
        psB = EN(nc.psum_tensor("psB", [128, 512], F32))
        psG = EN(nc.psum_tensor("psG", [128, 128], F32))
        ones1 = rows[0:1, 0:128]
        p.dma(identt[:], cns)
        p.dma(rows[:], rws)
        p.dma(brow[0:1, :], bproj[None, :])
        for i in range(8):
            p.dma(rw[:, i * 8:(i + 1) * 8], routw[i * 128:(i + 1) * 128, :])
            p.dma(sgws[:, i:i + 1], sgw[i * 128:(i + 1) * 128, :])
        for g in range(TOK // QT):
            t0 = g * QT
            for i in range(8):
                p.dma(big[:, i * 1024:(i + 1) * 1024], wprojT[i * 128:(i + 1) * 128, :])
            for c in range(nch):
                r0 = t0 + c * 128
                p.dma(s1[:], xsh[r0:r0 + 128, :])
                p.dma(s2[:], ash[r0:r0 + 128, :])
                emit_tp8(p, s2, s3, ps_t, identt[:])
                for h in range(2):
                    ps = psA if h == 0 else psB
                    for i in range(8):
                        p.mm(ps[:], s3[:, i * 128:(i + 1) * 128],
                             big[:, i * 1024 + h * 512: i * 1024 + h * 512 + 512],
                             start=(i == 0), stop=False)
                    p.mm(ps[:], ones1, brow[0:1, h * 512:h * 512 + 512],
                         start=False, stop=True)
                    p.tt(s4[:, h * 512:h * 512 + 512], s1[:, h * 512:h * 512 + 512],
                         ps[:], ALU.add)
                p.dma(x1scr[r0:r0 + 128, :], s4[:])
                emit_ln(p, s4, s2, s5, s6, mrow, mrow)
                emit_tp8(p, s6, lnTg[:, c * 1024:(c + 1) * 1024], ps_t, identt[:])
                for i in range(8):
                    p.mm(psG[:, 0:8], lnTg[:, c * 1024 + i * 128:c * 1024 + i * 128 + 128],
                         rw[:, i * 8:(i + 1) * 8], start=(i == 0), stop=(i == 7))
                emit_router(p, psG, pscr, comb[:, c * 8:(c + 1) * 8], mrow)
            for i in range(8):
                p.dma(big[:, i * 2048:(i + 1) * 2048], sgu[i * 128:(i + 1) * 128, :])
                p.dma(wdn[:, i * 1024:(i + 1) * 1024], sdn[i * 128:(i + 1) * 128, :])
            for c in range(nch):
                lt = lnTg[:, c * 1024:(c + 1) * 1024]
                for q in range(4):
                    for i in range(8):
                        p.mm(psA[:], lt[:, i * 128:(i + 1) * 128],
                             big[:, i * 2048 + q * 512: i * 2048 + q * 512 + 512],
                             start=(i == 0), stop=(i == 7))
                    if q < 2:
                        p.act(s1[:, q * 512:(q + 1) * 512], psA[:], AF.Silu)
                    else:
                        p.cp(s2[:, (q - 2) * 512:(q - 1) * 512], psA[:])
                p.tt(s3[:], s1[:], s2[:], ALU.mult)
                emit_tp8(p, s3, s4, ps_t, identt[:])
                for i in range(8):
                    p.mm(psG[:, 0:1], lt[:, i * 128:(i + 1) * 128], sgws[:, i:i + 1],
                         start=(i == 0), stop=(i == 7))
                p.act(mrow[:, 6:7], psG[:, 0:1], AF.Sigmoid)
                for h in range(2):
                    ps = psA if h == 0 else psB
                    for i in range(8):
                        p.mm(ps[:], s4[:, i * 128:(i + 1) * 128],
                             wdn[:, i * 1024 + h * 512:i * 1024 + h * 512 + 512],
                             start=(i == 0), stop=(i == 7))
                    p.ts(accg[:, c * 1024 + h * 512:c * 1024 + h * 512 + 512],
                         ps[:], mrow[:, 6:7], None, ALU.mult)
            for e in range(8):
                for i in range(8):
                    p.dma(big[:, i * 1024:(i + 1) * 1024], w1[e, i * 128:(i + 1) * 128, :])
                    p.dma(big[:, 8192 + i * 1024:8192 + (i + 1) * 1024],
                          w2[e, i * 128:(i + 1) * 128, :])
                for c in range(nch):
                    lt = lnTg[:, c * 1024:(c + 1) * 1024]
                    for h in range(2):
                        ps = psA if h == 0 else psB
                        for i in range(8):
                            p.mm(ps[:], lt[:, i * 128:(i + 1) * 128],
                                 big[:, i * 1024 + h * 512:i * 1024 + h * 512 + 512],
                                 start=(i == 0), stop=(i == 7))
                        p.act(s1[:, h * 512:(h + 1) * 512], ps[:], AF.Silu)
                    emit_tp8(p, s1, s2, ps_t, identt[:])
                    for h in range(2):
                        ps = psA if h == 0 else psB
                        for i in range(8):
                            p.mm(ps[:], s2[:, i * 128:(i + 1) * 128],
                                 big[:, 8192 + i * 1024 + h * 512:8192 + i * 1024 + h * 512 + 512],
                                 start=(i == 0), stop=(i == 7))
                        a = accg[:, c * 1024 + h * 512:c * 1024 + h * 512 + 512]
                        p.stt(a, ps[:], comb[:, c * 8 + e:c * 8 + e + 1], a,
                              ALU.mult, ALU.add)
            for c in range(nch):
                r0 = t0 + c * 128
                p.dma(s1[:], x1scr[r0:r0 + 128, :])
                p.tt(s2[:], s1[:], accg[:, c * 1024:(c + 1) * 1024], ALU.add)
                p.dma(x2o[r0:r0 + 128, :], s2[:])
        p.build_block(nc)
    return nc


def build_C():
    nc = bass.Bass("TRN2", target_bir_lowering=False)
    p = Prog()
    EXT = 2560
    NH = 16
    HD = 64
    xe = nc.dram_tensor("xe", [EXT, D], F32, kind="ExternalInput").ap()
    wqT = nc.dram_tensor("wqT", [D, D], F32, kind="ExternalInput").ap()
    wkT = nc.dram_tensor("wkT", [D, HD], F32, kind="ExternalInput").ap()
    wvT = nc.dram_tensor("wvT", [D, HD], F32, kind="ExternalInput").ap()
    wfT = nc.dram_tensor("wfT", [D, D], F32, kind="ExternalInput").ap()
    bf = nc.dram_tensor("bf", [D], F32, kind="ExternalInput").ap()
    msk = nc.dram_tensor("msk", [16, 128, 640], F32, kind="ExternalInput").ap()
    cns = nc.dram_tensor("cns", [128, 128], F32, kind="ExternalInput").ap()
    rws = nc.dram_tensor("rws", [1, 128], F32, kind="ExternalInput").ap()
    x3o = nc.dram_tensor("x3o", [2048, D], F32, kind="ExternalOutput").ap()

    st = ExitStack()
    with st:
        EN = st.enter_context
        identt = EN(nc.sbuf_tensor("ident", [128, 128], F32))
        rows = EN(nc.sbuf_tensor("rows", [1, 128], F32))
        brow = EN(nc.sbuf_tensor("brow", [1, D], F32))
        wq = EN(nc.sbuf_tensor("wq", [128, 8192], F32))
        wf = EN(nc.sbuf_tensor("wf", [128, 8192], F32))
        wk = EN(nc.sbuf_tensor("wk", [128, 512], F32))
        wv = EN(nc.sbuf_tensor("wv", [128, 512], F32))
        kTb = EN(nc.sbuf_tensor("kT", [64, 2560], F32))
        vb = EN(nc.sbuf_tensor("vb", [128, 1280], F32))
        qTc = EN(nc.sbuf_tensor("qT", [64, 2048], F32))
        mks = EN(nc.sbuf_tensor("mk", [128, 640], F32))
        lg = EN(nc.sbuf_tensor("lg", [128, 640], F32))
        pb = EN(nc.sbuf_tensor("pb", [128, 640], F32))
        pT = EN(nc.sbuf_tensor("pT", [128, 640], F32))
        s1 = EN(nc.sbuf_tensor("s1", [128, D], F32))
        s2 = EN(nc.sbuf_tensor("s2", [128, D], F32))
        s3 = EN(nc.sbuf_tensor("s3", [128, D], F32))
        s4 = EN(nc.sbuf_tensor("s4", [128, D], F32))
        at = EN(nc.sbuf_tensor("at", [128, D], F32))
        mrow = EN(nc.sbuf_tensor("mrow", [128, 8], F32))
        ps_t = EN(nc.psum_tensor("ps_t", [128, 128], F32))
        psL = EN(nc.psum_tensor("psL", [128, 128], F32))
        psO = EN(nc.psum_tensor("psO", [128, 64], F32))
        psF = EN(nc.psum_tensor("psF", [128, 512], F32))
        p.dma(identt[:], cns)
        p.dma(rows[:], rws)
        ones1 = rows[0:1, 0:128]
        p.dma(brow[0:1, :], bf[None, :])
        for i in range(8):
            p.dma(wq[:, i * 1024:(i + 1) * 1024], wqT[i * 128:(i + 1) * 128, :])
            p.dma(wf[:, i * 1024:(i + 1) * 1024], wfT[i * 128:(i + 1) * 128, :])
            p.dma(wk[:, i * 64:(i + 1) * 64], wkT[i * 128:(i + 1) * 128, :])
            p.dma(wv[:, i * 64:(i + 1) * 64], wvT[i * 128:(i + 1) * 128, :])
        for ec in range(EXT // 128):
            p.dma(s1[:], xe[ec * 128:(ec + 1) * 128, :])
            emit_ln(p, s1, s2, s3, s4, mrow, mrow)
            emit_tp8(p, s4, s3, ps_t, identt[:])
            for i in range(8):
                p.mm(psL[0:64, 0:128], wk[:, i * 64:(i + 1) * 64],
                     s3[:, i * 128:(i + 1) * 128], start=(i == 0), stop=(i == 7))
            p.cp(kTb[0:64, ec * 128:(ec + 1) * 128], psL[0:64, 0:128])
            for i in range(8):
                p.mm(psO[:, 0:64], s3[:, i * 128:(i + 1) * 128],
                     wv[:, i * 64:(i + 1) * 64], start=(i == 0), stop=(i == 7))
            p.cp(vb[:, ec * 64:(ec + 1) * 64], psO[:, 0:64])
        for c in range(16):
            p.dma(mks[:], msk[c, :, :])
            p.dma(s1[:], xe[(c + 2) * 128:(c + 3) * 128, :])
            emit_ln(p, s1, s2, s3, s4, mrow, mrow)
            emit_tp8(p, s4, s3, ps_t, identt[:])
            for n in range(NH):
                for i in range(8):
                    p.mm(psL[0:64, 0:128], wq[:, i * 1024 + n * 64:i * 1024 + n * 64 + 64],
                         s3[:, i * 128:(i + 1) * 128], start=(i == 0), stop=(i == 7))
                p.cp(qTc[0:64, n * 128:(n + 1) * 128], psL[0:64, 0:128])
            for n in range(NH):
                lhs_q = qTc[0:64, n * 128:n * 128 + 128]
                for kb in range(5):
                    eb = c + kb
                    p.mm(psL[:], lhs_q, kTb[0:64, eb * 128:(eb + 1) * 128],
                         start=True, stop=True)
                    p.stt(lg[:, kb * 128:(kb + 1) * 128], psL[:], 0.125,
                          mks[:, kb * 128:(kb + 1) * 128], ALU.mult, ALU.add)
                p.rmax(mrow[:, 0:1], lg[:])
                p.ts(mrow[:, 1:2], mrow[:, 0:1], -1.0, None, ALU.mult)
                p.act(pb[:], lg[:], AF.Exp, bias=mrow[:, 1:2])
                p.rsum(mrow[:, 2:3], pb[:])
                p.recip(mrow[:, 3:4], mrow[:, 2:3])
                p.ts(lg[:], pb[:], mrow[:, 3:4], None, ALU.mult)
                for kb in range(5):
                    p.tp(ps_t[:], lg[:, kb * 128:(kb + 1) * 128], identt[:])
                    p.cp(pT[:, kb * 128:(kb + 1) * 128], ps_t[:])
                for kb in range(5):
                    eb = c + kb
                    p.mm(psO[:, 0:64], pT[:, kb * 128:(kb + 1) * 128],
                         vb[:, eb * 64:(eb + 1) * 64], start=(kb == 0), stop=(kb == 4))
                p.cp(at[:, n * 64:(n + 1) * 64], psO[:, 0:64])
            emit_tp8(p, at, s2, ps_t, identt[:])
            for h in range(2):
                for i in range(8):
                    p.mm(psF[:], s2[:, i * 128:(i + 1) * 128],
                         wf[:, i * 1024 + h * 512:i * 1024 + h * 512 + 512],
                         start=(i == 0), stop=False)
                p.mm(psF[:], ones1, brow[0:1, h * 512:h * 512 + 512], start=False, stop=True)
                p.cp(s3[:, h * 512:(h + 1) * 512], psF[:])
            p.tt(s4[:], s1[:], s3[:], ALU.add)
            p.dma(x3o[c * 128:(c + 1) * 128, :], s4[:])
        p.build_block(nc)
    return nc


def build_D():
    nc = bass.Bass("TRN2", target_bir_lowering=False)
    p = Prog()
    TOK = 2048
    QT = 512
    FH = 4096
    x3 = nc.dram_tensor("x3", [TOK, D], F32, kind="ExternalInput").ap()
    wgT = nc.dram_tensor("wgT", [D, 8], F32, kind="ExternalInput").ap()
    W1 = nc.dram_tensor("W1", [8, D, FH], F32, kind="ExternalInput").ap()
    b1 = nc.dram_tensor("b1", [8, FH], F32, kind="ExternalInput").ap()
    W2 = nc.dram_tensor("W2", [8, FH, D], F32, kind="ExternalInput").ap()
    b2 = nc.dram_tensor("b2", [8, D], F32, kind="ExternalInput").ap()
    cns = nc.dram_tensor("cns", [128, 128], F32, kind="ExternalInput").ap()
    rws = nc.dram_tensor("rws", [1, 128], F32, kind="ExternalInput").ap()
    x4o = nc.dram_tensor("x4o", [TOK, D], F32, kind="ExternalOutput").ap()

    nch = QT // 128
    st = ExitStack()
    with st:
        EN = st.enter_context
        identt = EN(nc.sbuf_tensor("ident", [128, 128], F32))
        rows = EN(nc.sbuf_tensor("rows", [1, 128], F32))
        wg = EN(nc.sbuf_tensor("wg", [128, 64], F32))
        w1s = EN(nc.sbuf_tensor("w1s", [128, 8192], F32))
        w2s = EN(nc.sbuf_tensor("w2s", [128, 8192], F32))
        b1r = EN(nc.sbuf_tensor("b1r", [1, FH], F32))
        b2r = EN(nc.sbuf_tensor("b2r", [1, D], F32))
        lnTg = EN(nc.sbuf_tensor("lnTg", [128, QT * 8], F32))
        accg = EN(nc.sbuf_tensor("accg", [128, QT * 8], F32))
        yac = EN(nc.sbuf_tensor("yac", [128, QT * 8], F32))
        s1 = EN(nc.sbuf_tensor("s1", [128, D], F32))
        s2 = EN(nc.sbuf_tensor("s2", [128, D], F32))
        s3 = EN(nc.sbuf_tensor("s3", [128, D], F32))
        s4 = EN(nc.sbuf_tensor("s4", [128, D], F32))
        mrow = EN(nc.sbuf_tensor("mrow", [128, 8], F32))
        pscr = EN(nc.sbuf_tensor("pscr", [128, 40], F32))
        comb = EN(nc.sbuf_tensor("comb", [128, 8 * (QT // 128)], F32))
        ps_t = EN(nc.psum_tensor("ps_t", [128, 128], F32))
        psA = EN(nc.psum_tensor("psA", [128, 512], F32))
        psB = EN(nc.psum_tensor("psB", [128, 512], F32))
        psG = EN(nc.psum_tensor("psG", [128, 128], F32))
        ones1 = rows[0:1, 0:128]
        p.dma(identt[:], cns)
        p.dma(rows[:], rws)
        for i in range(8):
            p.dma(wg[:, i * 8:(i + 1) * 8], wgT[i * 128:(i + 1) * 128, :])
        for g in range(TOK // QT):
            t0 = g * QT
            for c in range(nch):
                r0 = t0 + c * 128
                p.dma(s1[:], x3[r0:r0 + 128, :])
                emit_ln(p, s1, s2, s3, s4, mrow, mrow)
                emit_tp8(p, s4, lnTg[:, c * 1024:(c + 1) * 1024], ps_t, identt[:])
                for i in range(8):
                    p.mm(psG[:, 0:8], lnTg[:, c * 1024 + i * 128:c * 1024 + i * 128 + 128],
                         wg[:, i * 8:(i + 1) * 8], start=(i == 0), stop=(i == 7))
                emit_router(p, psG, pscr, comb[:, c * 8:(c + 1) * 8], mrow)
                p.memset(accg[:, c * 1024:(c + 1) * 1024], 0.0)
            for e in range(8):
                p.dma(b1r[0:1, :], b1[e:e + 1, :])
                p.dma(b2r[0:1, :], b2[e:e + 1, :])
                for hq in range(4):
                    for i in range(8):
                        p.dma(w1s[:, i * 1024:(i + 1) * 1024],
                              W1[e, i * 128:(i + 1) * 128, hq * 1024:(hq + 1) * 1024])
                    w2q = W2[e, hq * 1024:(hq + 1) * 1024, :]
                    for j in range(8):
                        p.dma(w2s[:, j * 1024:(j + 1) * 1024],
                              w2q[j * 128:(j + 1) * 128, :])
                    for c in range(nch):
                        lt = lnTg[:, c * 1024:(c + 1) * 1024]
                        for h in range(2):
                            ps = psA if h == 0 else psB
                            for i in range(8):
                                p.mm(ps[:], lt[:, i * 128:(i + 1) * 128],
                                     w1s[:, i * 1024 + h * 512:i * 1024 + h * 512 + 512],
                                     start=(i == 0), stop=False)
                            p.mm(ps[:], ones1,
                                 b1r[0:1, hq * 1024 + h * 512:hq * 1024 + h * 512 + 512],
                                 start=False, stop=True)
                            p.act(s1[:, h * 512:(h + 1) * 512], ps[:], AF.Gelu)
                        emit_tp8(p, s1, s2, ps_t, identt[:])
                        for h in range(2):
                            ps = psA if h == 0 else psB
                            for j in range(8):
                                p.mm(ps[:], s2[:, j * 128:(j + 1) * 128],
                                     w2s[:, j * 1024 + h * 512:j * 1024 + h * 512 + 512],
                                     start=(j == 0), stop=(hq == 3))
                            y = yac[:, c * 1024 + h * 512:c * 1024 + h * 512 + 512]
                            if hq == 0:
                                p.cp(y, ps[:])
                            elif hq < 3:
                                p.tt(y, y, ps[:], ALU.add)
                            else:
                                p.tt(y, y, ps[:], ALU.add)
                                a = accg[:, c * 1024 + h * 512:c * 1024 + h * 512 + 512]
                                p.mm(ps[:], ones1,
                                     b2r[0:1, h * 512:h * 512 + 512], start=True, stop=True)
                                p.tt(y, y, ps[:], ALU.add)
                                p.stt(a, y, comb[:, c * 8 + e:c * 8 + e + 1], a,
                                      ALU.mult, ALU.add)
            for c in range(nch):
                r0 = t0 + c * 128
                p.dma(s1[:], x3[r0:r0 + 128, :])
                p.tt(s2[:], s1[:], accg[:, c * 1024:(c + 1) * 1024], ALU.add)
                p.dma(x4o[r0:r0 + 128, :], s2[:])
        p.build_block(nc)
    return nc


_DBG = {}


def _run(nc, in_maps, names):
    res = run_bass_kernel_spmd(nc, in_maps, core_ids=list(range(NCORE)))
    return [[np.asarray(r[n]) for n in names] for r in res.results]


def kernel(**inp):
    f = np.float32
    x = np.ascontiguousarray(inp["L0_attn_x"], dtype=f)
    ident = np.eye(128, dtype=f)
    cumA = np.triu(np.ones((128, 128), f))
    shiftA = np.eye(128, k=1, dtype=f)
    rowsA = np.zeros((1, 256), f)
    rowsA[0, 0:128] = 1.0
    rowsA[0, 128] = 1.0
    rows1 = np.ones((1, 128), f)
    cnsA = np.concatenate([ident, cumA, shiftA], axis=1)

    wqT = np.ascontiguousarray(inp["L0_attn_Wq1"].T, f)
    wkT = np.ascontiguousarray(inp["L0_attn_Wk1"].T, f)
    wk2T = np.ascontiguousarray(inp["L0_attn_Wk2"].T, f)

    ncA = build_A()
    maps = []
    for c in range(NCORE):
        b, g = c // 4, c % 4
        sel = g * 256
        maps.append({
            "xb": np.ascontiguousarray(np.roll(x[b], -sel, axis=1)),
            "wq": np.ascontiguousarray(np.roll(wqT[:, sel:sel + 256], -sel, axis=0)),
            "wk": np.ascontiguousarray(np.roll(wkT[:, sel:sel + 256], -sel, axis=0)),
            "wk2": np.ascontiguousarray(np.roll(wk2T[:, sel:sel + 256], -sel, axis=0)),
            "bq": np.ascontiguousarray(inp["L0_attn_bq1"][sel:sel + 256], f),
            "bk": np.ascontiguousarray(inp["L0_attn_bk1"][sel:sel + 256], f),
            "bk2": np.ascontiguousarray(inp["L0_attn_bk2"][sel:sel + 256], f),
            "cns": cnsA, "rws": rowsA,
        })
    outA = _run(ncA, maps, ["aout"])
    A = np.zeros((BATCH, SEQ, D), f)
    for c in range(NCORE):
        b, g = c // 4, c % 4
        A[b][:, g * 256:(g + 1) * 256] = outA[c][0]
    _DBG["A"] = A

    xf = x.reshape(NTOK, D)
    Af = A.reshape(NTOK, D)
    ncB = build_B()
    maps = []
    shared = {
        "wprojT": np.ascontiguousarray(inp["L0_attn_Wproj"].T, f),
        "bproj": np.ascontiguousarray(inp["L0_attn_bproj"], f),
        "routw": np.ascontiguousarray(inp["L0_ffn_router_w"], f),
        "w1": np.ascontiguousarray(inp["L0_ffn_w1"], f),
        "w2": np.ascontiguousarray(inp["L0_ffn_w2"], f),
        "sgu": np.ascontiguousarray(inp["L0_ffn_shared_gate_up"], f),
        "sdn": np.ascontiguousarray(inp["L0_ffn_shared_down"], f),
        "sgw": np.ascontiguousarray(inp["L0_ffn_shared_gate_w"], f),
        "cns": ident, "rws": rows1,
    }
    for c in range(NCORE):
        m = dict(shared)
        m["xsh"] = np.ascontiguousarray(xf[c * 2048:(c + 1) * 2048])
        m["ash"] = np.ascontiguousarray(Af[c * 2048:(c + 1) * 2048])
        maps.append(m)
    outB = _run(ncB, maps, ["x2o"])
    x2 = np.concatenate([o[0] for o in outB], axis=0)
    _DBG["x2"] = x2

    x2r = x2.reshape(BATCH, SEQ, D)
    ncC = build_C()
    maps = []
    sharedC = {
        "wqT": np.ascontiguousarray(inp["L1_attn_Wq"].T, f),
        "wkT": np.ascontiguousarray(inp["L1_attn_Wk"].T, f),
        "wvT": np.ascontiguousarray(inp["L1_attn_Wv"].T, f),
        "wfT": np.ascontiguousarray(inp["L1_attn_Wf"].T, f),
        "bf": np.ascontiguousarray(inp["L1_attn_bf"], f),
        "cns": ident, "rws": rows1,
    }
    for c in range(NCORE):
        b, gq = c // 4, c % 4
        qbase = gq * 2048
        ext = np.zeros((2560, D), f)
        lo = qbase - 256
        hi = qbase + 2048 + 256
        slo, shi = max(lo, 0), min(hi, SEQ)
        ext[slo - lo:shi - lo] = x2r[b, slo:shi]
        mask = np.full((16, 128, 640), -1e30, f)
        for cc in range(16):
            qpos = qbase + cc * 128 + np.arange(128)
            spos = lo + cc * 128 + np.arange(640)
            ok = (np.abs(qpos[:, None] - spos[None, :]) <= 256) & \
                 (spos[None, :] >= 0) & (spos[None, :] < SEQ)
            mask[cc][ok] = 0.0
        m = dict(sharedC)
        m["xe"] = ext
        m["msk"] = mask
        maps.append(m)
    outC = _run(ncC, maps, ["x3o"])
    x3 = np.concatenate([o[0] for o in outC], axis=0)
    _DBG["x3"] = x3

    ncD = build_D()
    maps = []
    sharedD = {
        "wgT": np.ascontiguousarray(inp["L1_ffn_w_gate"].T, f),
        "W1": np.ascontiguousarray(inp["L1_ffn_W1"], f),
        "b1": np.ascontiguousarray(inp["L1_ffn_b1"], f),
        "W2": np.ascontiguousarray(inp["L1_ffn_W2"], f),
        "b2": np.ascontiguousarray(inp["L1_ffn_b2"], f),
        "cns": ident, "rws": rows1,
    }
    for c in range(NCORE):
        m = dict(sharedD)
        m["x3"] = np.ascontiguousarray(x3[c * 2048:(c + 1) * 2048])
        maps.append(m)
    outD = _run(ncD, maps, ["x4o"])
    x4 = np.concatenate([o[0] for o in outD], axis=0)
    return x4.reshape(BATCH, SEQ, D)


# version 02
# speedup vs baseline: 1.0009x; 1.0009x over previous
from contextlib import ExitStack
import numpy as np
import concourse.bass as bass
import concourse.mybir as mybir
from concourse.bass_utils import run_bass_kernel_spmd

F32 = mybir.dt.float32
AF = mybir.ActivationFunctionType
ALU = mybir.AluOpType
AX = mybir.AxisListType.X

D = 1024
NCORE = 8
SEQ = 8192
BATCH = 2
NTOK = BATCH * SEQ


class Prog:

    def __init__(self):
        self.ops = []
        self.c_tot = 0
        self.d_tot = 0

    def _emit(self, eng, method, *args, **kwargs):
        is_dma = method == "dma_start"
        self.ops.append((eng, method, args, kwargs, self.c_tot, self.d_tot, is_dma))
        if is_dma:
            self.d_tot += 16
        else:
            self.c_tot += 1

    def dma(self, dst, src, **kw):
        self._emit("sync", "dma_start", out=dst, in_=src, **kw)

    def mm(self, out, lhsT, rhs, start, stop):
        self._emit("tensor", "matmul", out, lhsT, rhs, start=start, stop=stop)

    def tp(self, out, in_, ident):
        self._emit("tensor", "transpose", out, in_, ident)

    def act(self, out, in_, func, bias=0.0, scale=1.0):
        self._emit("scalar", "activation", out, in_, func, bias=bias, scale=scale)

    def s_add(self, out, in_, c):
        self._emit("scalar", "add", out, in_, c)

    def s_mul(self, out, in_, c):
        self._emit("scalar", "mul", out, in_, c)

    def cp(self, out, in_):
        self._emit("scalar", "copy", out, in_)

    def tt(self, out, a, b, op):
        self._emit("vector", "tensor_tensor", out, a, b, op)

    def ts(self, out, in0, s1, s2, op0, op1=None):
        if s2 is None and op1 is None:
            self._emit("vector", "tensor_single_scalar", out, in0, s1, op0)
        else:
            self._emit("vector", "tensor_scalar", out, in0, s1, s2, op0, op1)

    def stt(self, out, in0, scalar, in1, op0, op1):
        self._emit("vector", "scalar_tensor_tensor", out, in0, scalar, in1, op0, op1)

    def rsum(self, out, in_):
        self._emit("vector", "reduce_sum", out, in_, AX)

    def rmax(self, out, in_):
        self._emit("vector", "reduce_max", out, in_, AX)

    def recip(self, out, in_):
        self._emit("vector", "reciprocal", out, in_)

    def memset(self, ap, v):
        self._emit("vector", "memset", ap, v)

    def build_block(self, nc):
        with (
            nc.semaphore("csem") as csem,
            nc.semaphore("dsem") as dsem,
            nc.Block() as block,
        ):
            def mk(eng_name):
                def run(eng):
                    for (en, meth, args, kwargs, cw, dw, is_dma) in self.ops:
                        if en != eng_name:
                            continue
                        if cw > 0:
                            eng.wait_ge(csem, cw)
                        if dw > 0:
                            eng.wait_ge(dsem, dw)
                        instr = getattr(eng, meth)(*args, **kwargs)
                        instr.then_inc(dsem if is_dma else csem, 16 if is_dma else 1)
                    if eng_name == "sync":
                        eng.wait_ge(csem, self.c_tot)
                        eng.wait_ge(dsem, self.d_tot)
                return run

            block.sync(mk("sync"))
            block.tensor(mk("tensor"))
            block.scalar(mk("scalar"))
            block.vector(mk("vector"))


def emit_ln(p, xin, xc, scr, ln, mrow, vrow, width=D):
    p.rsum(mrow[:, 0:1], xin[:, 0:width])
    p.ts(mrow[:, 1:2], mrow[:, 0:1], -1.0 / width, None, ALU.mult)
    p.ts(xc[:, 0:width], xin[:, 0:width], mrow[:, 1:2], None, ALU.add)
    p.act(scr[:, 0:width], xc[:, 0:width], AF.Square)
    p.rsum(vrow[:, 0:1], scr[:, 0:width])
    p.ts(vrow[:, 1:2], vrow[:, 0:1], 1.0 / width, 1e-5, ALU.mult, ALU.add)
    p.act(vrow[:, 2:3], vrow[:, 1:2], AF.Sqrt)
    p.recip(vrow[:, 3:4], vrow[:, 2:3])
    p.ts(ln[:, 0:width], xc[:, 0:width], vrow[:, 3:4], None, ALU.mult)


def emit_tp8(p, ln, lnT, ps_t, ident, nblk=8):
    for i in range(nblk):
        p.tp(ps_t[:, 0:128], ln[:, i * 128:(i + 1) * 128], ident)
        p.cp(lnT[:, i * 128:(i + 1) * 128], ps_t[:, 0:128])


def emit_router(p, psG, probs_scr, comb_out, mrow):
    ex = probs_scr[:, 0:8]
    pr = probs_scr[:, 8:16]
    eq = probs_scr[:, 16:24]
    pw = probs_scr[:, 24:32]
    mk2 = probs_scr[:, 32:40]
    p.rmax(mrow[:, 0:1], psG[:, 0:8])
    p.ts(mrow[:, 1:2], mrow[:, 0:1], -1.0, None, ALU.mult)
    p.act(ex, psG[:, 0:8], AF.Exp, bias=mrow[:, 1:2])
    p.rsum(mrow[:, 2:3], ex)
    p.recip(mrow[:, 3:4], mrow[:, 2:3])
    p.ts(pr, ex, mrow[:, 3:4], None, ALU.mult)
    p.rmax(mrow[:, 4:5], pr)
    p.ts(eq, pr, mrow[:, 4:5], None, ALU.is_ge)
    p.ts(pw, eq, -1.0, 1.0, ALU.mult, ALU.add)
    p.tt(pw, pr, pw, ALU.mult)
    p.rmax(mrow[:, 5:6], pw)
    p.ts(mk2, pr, mrow[:, 5:6], None, ALU.is_ge)
    p.tt(comb_out, pr, mk2, ALU.mult)


def build_A():
    nc = bass.Bass("TRN2", target_bir_lowering=False)
    p = Prog()
    xb = nc.dram_tensor("xb", [SEQ, D], F32, kind="ExternalInput").ap()
    wq = nc.dram_tensor("wq", [D, 256], F32, kind="ExternalInput").ap()
    wk = nc.dram_tensor("wk", [D, 256], F32, kind="ExternalInput").ap()
    wk2 = nc.dram_tensor("wk2", [D, 256], F32, kind="ExternalInput").ap()
    bq = nc.dram_tensor("bq", [256], F32, kind="ExternalInput").ap()
    bk = nc.dram_tensor("bk", [256], F32, kind="ExternalInput").ap()
    bk2 = nc.dram_tensor("bk2", [256], F32, kind="ExternalInput").ap()
    cns = nc.dram_tensor("cns", [128, 384], F32, kind="ExternalInput").ap()
    rws = nc.dram_tensor("rws", [1, 256], F32, kind="ExternalInput").ap()
    aout = nc.dram_tensor("aout", [SEQ, 256], F32, kind="ExternalOutput").ap()

    st = ExitStack()
    with st:
        EN = st.enter_context
        consts = EN(nc.sbuf_tensor("consts", [128, 384], F32))
        rows = EN(nc.sbuf_tensor("rows", [1, 256], F32))
        brow = EN(nc.sbuf_tensor("brow", [1, 768], F32))
        wqs = EN(nc.sbuf_tensor("wqs", [128, 2048], F32))
        wks = EN(nc.sbuf_tensor("wks", [128, 2048], F32))
        wk2s = EN(nc.sbuf_tensor("wk2s", [128, 2048], F32))
        xin = EN(nc.sbuf_tensor("xin", [128, D], F32))
        xc = EN(nc.sbuf_tensor("xc0", [128, D], F32))
        scr = EN(nc.sbuf_tensor("scr", [128, D], F32))
        ln = EN(nc.sbuf_tensor("ln", [128, D], F32))
        lnT = EN(nc.sbuf_tensor("lnT", [128, D], F32))
        mrow = EN(nc.sbuf_tensor("mrow", [128, 8], F32))
        Q = EN(nc.sbuf_tensor("Q", [128, 256], F32))
        K = EN(nc.sbuf_tensor("K", [128, 256], F32))
        T2 = EN(nc.sbuf_tensor("T2", [128, 256], F32))
        KV = EN(nc.sbuf_tensor("KV", [128, 256], F32))
        Kc = EN(nc.sbuf_tensor("Kc", [128, 256], F32))
        rKc = EN(nc.sbuf_tensor("rKc", [128, 256], F32))
        S = EN(nc.sbuf_tensor("S", [128, 256], F32))
        Oar = EN(nc.sbuf_tensor("Oar", [128, 256], F32))
        E = EN(nc.sbuf_tensor("E", [128, 256], F32))
        K2 = EN(nc.sbuf_tensor("K2", [128, 256], F32))
        K2E = EN(nc.sbuf_tensor("K2E", [128, 256], F32))
        S2 = EN(nc.sbuf_tensor("S2", [128, 256], F32))
        Oma = EN(nc.sbuf_tensor("Oma", [128, 256], F32))
        Ao = EN(nc.sbuf_tensor("Ao", [128, 256], F32))
        carK = EN(nc.sbuf_tensor("carK", [1, 256], F32))
        carKV = EN(nc.sbuf_tensor("carKV", [1, 256], F32))
        carS2 = EN(nc.sbuf_tensor("carS2", [1, 256], F32))
        pOar = EN(nc.sbuf_tensor("pOar", [1, 256], F32))
        pT2 = EN(nc.sbuf_tensor("pT2", [1, 256], F32))
        pQ = EN(nc.sbuf_tensor("pQ", [1, 256], F32))
        cext = EN(nc.sbuf_tensor("cext", [128, 256], F32))
        ps_t = EN(nc.psum_tensor("ps_t", [128, 128], F32))
        psX = EN(nc.psum_tensor("psX", [128, 256], F32))
        ident = consts[:, 0:128]
        cumA = consts[:, 128:256]
        shiftA = consts[:, 256:384]
        ones1 = rows[0:1, 0:128]
        sel0 = rows[0:1, 128:256]
        p.dma(consts[:], cns)
        p.dma(rows[:], rws)
        p.dma(brow[0:1, 0:256], bq[None, :])
        p.dma(brow[0:1, 256:512], bk[None, :])
        p.dma(brow[0:1, 512:768], bk2[None, :])
        for i in range(8):
            p.dma(wqs[:, i * 256:(i + 1) * 256], wq[i * 128:(i + 1) * 128, :])
            p.dma(wks[:, i * 256:(i + 1) * 256], wk[i * 128:(i + 1) * 128, :])
            p.dma(wk2s[:, i * 256:(i + 1) * 256], wk2[i * 128:(i + 1) * 128, :])
        for t in (carK, carKV, carS2, pOar, pQ):
            p.memset(t[0:1, :], 0.0)
        p.ts(cext[0:1, 0:256], brow[0:1, 512:768], 6.25e-4, None, ALU.mult)
        p.act(pT2[0:1, :], cext[0:1, 0:256], AF.Sigmoid)

        for c in range(SEQ // 128):
            p.dma(xin[:], xb[c * 128:(c + 1) * 128, :])
            emit_ln(p, xin, xc, scr, ln, mrow, mrow)
            emit_tp8(p, ln, lnT, ps_t, ident)
            V = ln[:, 0:256]
            for (ws, bo, dst, fn, sc) in (
                (wqs, 0, Q, AF.Sigmoid, None),
                (wks, 256, K, AF.Exp, None),
                (wk2s, 512, T2, AF.Sigmoid, 6.25e-4),
            ):
                for i in range(8):
                    p.mm(psX[:], lnT[:, i * 128:(i + 1) * 128],
                         ws[:, i * 256:(i + 1) * 256], start=(i == 0), stop=False)
                p.mm(psX[:], ones1, brow[0:1, bo:bo + 256], start=False, stop=True)
                if sc is None:
                    p.act(dst[:], psX[:], fn)
                else:
                    p.ts(scr[:, 0:256], psX[:], sc, None, ALU.mult)
                    p.act(dst[:], scr[:, 0:256], fn)
            p.tt(KV[:], K[:], V, ALU.mult)
            p.mm(psX[:], cumA, K[:], start=True, stop=False)
            p.mm(psX[:], ones1, carK[0:1, :], start=False, stop=True)
            p.cp(cext[:], psX[:])
            p.dma(carK[0:1, :], cext[127:128, :])
            p.ts(Kc[:], psX[:], 1e-6, None, ALU.add)
            p.recip(rKc[:], Kc[:])
            p.mm(psX[:], cumA, KV[:], start=True, stop=False)
            p.mm(psX[:], ones1, carKV[0:1, :], start=False, stop=True)
            p.cp(cext[:], psX[:])
            p.dma(carKV[0:1, :], cext[127:128, :])
            p.tt(S[:], psX[:], rKc[:], ALU.mult)
            p.tt(Oar[:], S[:], Q[:], ALU.mult)
            p.mm(psX[:], shiftA, Oar[:], start=True, stop=False)
            p.mm(psX[:], sel0, pOar[0:1, :], start=False, stop=True)
            p.tt(E[:], V, psX[:], ALU.subtract)
            if c == 0:
                p.memset(E[0:1, :], 0.0)
            p.dma(pOar[0:1, :], Oar[127:128, :])
            p.mm(psX[:], shiftA, T2[:], start=True, stop=False)
            p.mm(psX[:], sel0, pT2[0:1, :], start=False, stop=True)
            p.cp(K2[:], psX[:])
            p.dma(pT2[0:1, :], T2[127:128, :])
            p.tt(K2E[:], K2[:], E[:], ALU.mult)
            p.mm(psX[:], cumA, K2E[:], start=True, stop=False)
            p.mm(psX[:], ones1, carS2[0:1, :], start=False, stop=True)
            p.cp(S2[:], psX[:])
            p.dma(carS2[0:1, :], S2[127:128, :])
            p.mm(psX[:], shiftA, Q[:], start=True, stop=False)
            p.mm(psX[:], sel0, pQ[0:1, :], start=False, stop=True)
            p.stt(Oma[:], psX[:], 6.25e-4, S2[:], ALU.mult, ALU.mult)
            p.dma(pQ[0:1, :], Q[127:128, :])
            p.tt(Ao[:], Oar[:], Oma[:], ALU.add)
            p.dma(aout[c * 128:(c + 1) * 128, :], Ao[:])
        p.build_block(nc)
    return nc


def build_B():
    nc = bass.Bass("TRN2", target_bir_lowering=False)
    p = Prog()
    TOK = 2048
    QT = 512
    xsh = nc.dram_tensor("xsh", [TOK, D], F32, kind="ExternalInput").ap()
    ash = nc.dram_tensor("ash", [TOK, D], F32, kind="ExternalInput").ap()
    wprojT = nc.dram_tensor("wprojT", [D, D], F32, kind="ExternalInput").ap()
    bproj = nc.dram_tensor("bproj", [D], F32, kind="ExternalInput").ap()
    routw = nc.dram_tensor("routw", [D, 8], F32, kind="ExternalInput").ap()
    w1 = nc.dram_tensor("w1", [8, D, D], F32, kind="ExternalInput").ap()
    w2 = nc.dram_tensor("w2", [8, D, D], F32, kind="ExternalInput").ap()
    sgu = nc.dram_tensor("sgu", [D, 2048], F32, kind="ExternalInput").ap()
    sdn = nc.dram_tensor("sdn", [D, D], F32, kind="ExternalInput").ap()
    sgw = nc.dram_tensor("sgw", [D, 1], F32, kind="ExternalInput").ap()
    cns = nc.dram_tensor("cns", [128, 128], F32, kind="ExternalInput").ap()
    rws = nc.dram_tensor("rws", [1, 128], F32, kind="ExternalInput").ap()
    x1scr = nc.dram_tensor("x1scr", [TOK, D], F32, kind="Internal").ap()
    x2o = nc.dram_tensor("x2o", [TOK, D], F32, kind="ExternalOutput").ap()

    nch = QT // 128
    st = ExitStack()
    with st:
        EN = st.enter_context
        identt = EN(nc.sbuf_tensor("ident", [128, 128], F32))
        rows = EN(nc.sbuf_tensor("rows", [1, 128], F32))
        brow = EN(nc.sbuf_tensor("brow", [1, D], F32))
        big = EN(nc.sbuf_tensor("big", [128, 16384], F32))
        wdn = EN(nc.sbuf_tensor("wdn", [128, 8192], F32))
        rw = EN(nc.sbuf_tensor("rw", [128, 64], F32))
        sgws = EN(nc.sbuf_tensor("sgws", [128, 8], F32))
        lnTg = EN(nc.sbuf_tensor("lnTg", [128, 4096], F32))
        accg = EN(nc.sbuf_tensor("accg", [128, 4096], F32))
        s1 = EN(nc.sbuf_tensor("s1", [128, D], F32))
        s2 = EN(nc.sbuf_tensor("s2", [128, D], F32))
        s3 = EN(nc.sbuf_tensor("s3", [128, D], F32))
        s4 = EN(nc.sbuf_tensor("s4", [128, D], F32))
        s5 = EN(nc.sbuf_tensor("s5", [128, D], F32))
        s6 = EN(nc.sbuf_tensor("s6", [128, D], F32))
        mrow = EN(nc.sbuf_tensor("mrow", [128, 8], F32))
        pscr = EN(nc.sbuf_tensor("pscr", [128, 40], F32))
        comb = EN(nc.sbuf_tensor("comb", [128, 32], F32))
        ps_t = EN(nc.psum_tensor("ps_t", [128, 128], F32))
        psA = EN(nc.psum_tensor("psA", [128, 512], F32))
        psB = EN(nc.psum_tensor("psB", [128, 512], F32))
        psG = EN(nc.psum_tensor("psG", [128, 128], F32))
        ones1 = rows[0:1, 0:128]
        p.dma(identt[:], cns)
        p.dma(rows[:], rws)
        p.dma(brow[0:1, :], bproj[None, :])
        for i in range(8):
            p.dma(rw[:, i * 8:(i + 1) * 8], routw[i * 128:(i + 1) * 128, :])
            p.dma(sgws[:, i:i + 1], sgw[i * 128:(i + 1) * 128, :])
        for g in range(TOK // QT):
            t0 = g * QT
            for i in range(8):
                p.dma(big[:, i * 1024:(i + 1) * 1024], wprojT[i * 128:(i + 1) * 128, :])
            for c in range(nch):
                r0 = t0 + c * 128
                p.dma(s1[:], xsh[r0:r0 + 128, :])
                p.dma(s2[:], ash[r0:r0 + 128, :])
                emit_tp8(p, s2, s3, ps_t, identt[:])
                for h in range(2):
                    ps = psA if h == 0 else psB
                    for i in range(8):
                        p.mm(ps[:], s3[:, i * 128:(i + 1) * 128],
                             big[:, i * 1024 + h * 512: i * 1024 + h * 512 + 512],
                             start=(i == 0), stop=False)
                    p.mm(ps[:], ones1, brow[0:1, h * 512:h * 512 + 512],
                         start=False, stop=True)
                    p.tt(s4[:, h * 512:h * 512 + 512], s1[:, h * 512:h * 512 + 512],
                         ps[:], ALU.add)
                p.dma(x1scr[r0:r0 + 128, :], s4[:])
                emit_ln(p, s4, s2, s5, s6, mrow, mrow)
                emit_tp8(p, s6, lnTg[:, c * 1024:(c + 1) * 1024], ps_t, identt[:])
                for i in range(8):
                    p.mm(psG[:, 0:8], lnTg[:, c * 1024 + i * 128:c * 1024 + i * 128 + 128],
                         rw[:, i * 8:(i + 1) * 8], start=(i == 0), stop=(i == 7))
                emit_router(p, psG, pscr, comb[:, c * 8:(c + 1) * 8], mrow)
            for i in range(8):
                p.dma(big[:, i * 2048:(i + 1) * 2048], sgu[i * 128:(i + 1) * 128, :])
                p.dma(wdn[:, i * 1024:(i + 1) * 1024], sdn[i * 128:(i + 1) * 128, :])
            for c in range(nch):
                lt = lnTg[:, c * 1024:(c + 1) * 1024]
                for q in range(4):
                    for i in range(8):
                        p.mm(psA[:], lt[:, i * 128:(i + 1) * 128],
                             big[:, i * 2048 + q * 512: i * 2048 + q * 512 + 512],
                             start=(i == 0), stop=(i == 7))
                    if q < 2:
                        p.act(s1[:, q * 512:(q + 1) * 512], psA[:], AF.Silu)
                    else:
                        p.cp(s2[:, (q - 2) * 512:(q - 1) * 512], psA[:])
                p.tt(s3[:], s1[:], s2[:], ALU.mult)
                emit_tp8(p, s3, s4, ps_t, identt[:])
                for i in range(8):
                    p.mm(psG[:, 0:1], lt[:, i * 128:(i + 1) * 128], sgws[:, i:i + 1],
                         start=(i == 0), stop=(i == 7))
                p.act(mrow[:, 6:7], psG[:, 0:1], AF.Sigmoid)
                for h in range(2):
                    ps = psA if h == 0 else psB
                    for i in range(8):
                        p.mm(ps[:], s4[:, i * 128:(i + 1) * 128],
                             wdn[:, i * 1024 + h * 512:i * 1024 + h * 512 + 512],
                             start=(i == 0), stop=(i == 7))
                    p.ts(accg[:, c * 1024 + h * 512:c * 1024 + h * 512 + 512],
                         ps[:], mrow[:, 6:7], None, ALU.mult)
            for e in range(8):
                for i in range(8):
                    p.dma(big[:, i * 1024:(i + 1) * 1024], w1[e, i * 128:(i + 1) * 128, :])
                    p.dma(big[:, 8192 + i * 1024:8192 + (i + 1) * 1024],
                          w2[e, i * 128:(i + 1) * 128, :])
                for c in range(nch):
                    lt = lnTg[:, c * 1024:(c + 1) * 1024]
                    for h in range(2):
                        ps = psA if h == 0 else psB
                        for i in range(8):
                            p.mm(ps[:], lt[:, i * 128:(i + 1) * 128],
                                 big[:, i * 1024 + h * 512:i * 1024 + h * 512 + 512],
                                 start=(i == 0), stop=(i == 7))
                        p.act(s1[:, h * 512:(h + 1) * 512], ps[:], AF.Silu)
                    emit_tp8(p, s1, s2, ps_t, identt[:])
                    for h in range(2):
                        ps = psA if h == 0 else psB
                        for i in range(8):
                            p.mm(ps[:], s2[:, i * 128:(i + 1) * 128],
                                 big[:, 8192 + i * 1024 + h * 512:8192 + i * 1024 + h * 512 + 512],
                                 start=(i == 0), stop=(i == 7))
                        a = accg[:, c * 1024 + h * 512:c * 1024 + h * 512 + 512]
                        p.stt(a, ps[:], comb[:, c * 8 + e:c * 8 + e + 1], a,
                              ALU.mult, ALU.add)
            for c in range(nch):
                r0 = t0 + c * 128
                p.dma(s1[:], x1scr[r0:r0 + 128, :])
                p.tt(s2[:], s1[:], accg[:, c * 1024:(c + 1) * 1024], ALU.add)
                p.dma(x2o[r0:r0 + 128, :], s2[:])
        p.build_block(nc)
    return nc


def build_C():
    nc = bass.Bass("TRN2", target_bir_lowering=False)
    p = Prog()
    EXT = 2560
    NH = 16
    HD = 64
    xe = nc.dram_tensor("xe", [EXT, D], F32, kind="ExternalInput").ap()
    wqT = nc.dram_tensor("wqT", [D, D], F32, kind="ExternalInput").ap()
    wkT = nc.dram_tensor("wkT", [D, HD], F32, kind="ExternalInput").ap()
    wvT = nc.dram_tensor("wvT", [D, HD], F32, kind="ExternalInput").ap()
    wfT = nc.dram_tensor("wfT", [D, D], F32, kind="ExternalInput").ap()
    bf = nc.dram_tensor("bf", [D], F32, kind="ExternalInput").ap()
    msk = nc.dram_tensor("msk", [16, 128, 640], F32, kind="ExternalInput").ap()
    cns = nc.dram_tensor("cns", [128, 128], F32, kind="ExternalInput").ap()
    rws = nc.dram_tensor("rws", [1, 128], F32, kind="ExternalInput").ap()
    x3o = nc.dram_tensor("x3o", [2048, D], F32, kind="ExternalOutput").ap()

    st = ExitStack()
    with st:
        EN = st.enter_context
        identt = EN(nc.sbuf_tensor("ident", [128, 128], F32))
        rows = EN(nc.sbuf_tensor("rows", [1, 128], F32))
        brow = EN(nc.sbuf_tensor("brow", [1, D], F32))
        wq = EN(nc.sbuf_tensor("wq", [128, 8192], F32))
        wf = EN(nc.sbuf_tensor("wf", [128, 8192], F32))
        wk = EN(nc.sbuf_tensor("wk", [128, 512], F32))
        wv = EN(nc.sbuf_tensor("wv", [128, 512], F32))
        kTb = EN(nc.sbuf_tensor("kT", [64, 2560], F32))
        vb = EN(nc.sbuf_tensor("vb", [128, 1280], F32))
        qTc = EN(nc.sbuf_tensor("qT", [64, 2048], F32))
        mks = EN(nc.sbuf_tensor("mk", [128, 640], F32))
        lg = EN(nc.sbuf_tensor("lg", [128, 640], F32))
        pb = EN(nc.sbuf_tensor("pb", [128, 640], F32))
        pT = EN(nc.sbuf_tensor("pT", [128, 640], F32))
        s1 = EN(nc.sbuf_tensor("s1", [128, D], F32))
        s2 = EN(nc.sbuf_tensor("s2", [128, D], F32))
        s3 = EN(nc.sbuf_tensor("s3", [128, D], F32))
        s4 = EN(nc.sbuf_tensor("s4", [128, D], F32))
        at = EN(nc.sbuf_tensor("at", [128, D], F32))
        mrow = EN(nc.sbuf_tensor("mrow", [128, 8], F32))
        ps_t = EN(nc.psum_tensor("ps_t", [128, 128], F32))
        psL = EN(nc.psum_tensor("psL", [128, 128], F32))
        psO = EN(nc.psum_tensor("psO", [128, 64], F32))
        psF = EN(nc.psum_tensor("psF", [128, 512], F32))
        p.dma(identt[:], cns)
        p.dma(rows[:], rws)
        ones1 = rows[0:1, 0:128]
        p.dma(brow[0:1, :], bf[None, :])
        for i in range(8):
            p.dma(wq[:, i * 1024:(i + 1) * 1024], wqT[i * 128:(i + 1) * 128, :])
            p.dma(wf[:, i * 1024:(i + 1) * 1024], wfT[i * 128:(i + 1) * 128, :])
            p.dma(wk[:, i * 64:(i + 1) * 64], wkT[i * 128:(i + 1) * 128, :])
            p.dma(wv[:, i * 64:(i + 1) * 64], wvT[i * 128:(i + 1) * 128, :])
        for ec in range(EXT // 128):
            p.dma(s1[:], xe[ec * 128:(ec + 1) * 128, :])
            emit_ln(p, s1, s2, s3, s4, mrow, mrow)
            emit_tp8(p, s4, s3, ps_t, identt[:])
            for i in range(8):
                p.mm(psL[0:64, 0:128], wk[:, i * 64:(i + 1) * 64],
                     s3[:, i * 128:(i + 1) * 128], start=(i == 0), stop=(i == 7))
            p.cp(kTb[0:64, ec * 128:(ec + 1) * 128], psL[0:64, 0:128])
            for i in range(8):
                p.mm(psO[:, 0:64], s3[:, i * 128:(i + 1) * 128],
                     wv[:, i * 64:(i + 1) * 64], start=(i == 0), stop=(i == 7))
            p.cp(vb[:, ec * 64:(ec + 1) * 64], psO[:, 0:64])
        for c in range(16):
            p.dma(mks[:], msk[c, :, :])
            p.dma(s1[:], xe[(c + 2) * 128:(c + 3) * 128, :])
            emit_ln(p, s1, s2, s3, s4, mrow, mrow)
            emit_tp8(p, s4, s3, ps_t, identt[:])
            for n in range(NH):
                for i in range(8):
                    p.mm(psL[0:64, 0:128], wq[:, i * 1024 + n * 64:i * 1024 + n * 64 + 64],
                         s3[:, i * 128:(i + 1) * 128], start=(i == 0), stop=(i == 7))
                p.cp(qTc[0:64, n * 128:(n + 1) * 128], psL[0:64, 0:128])
            for n in range(NH):
                lhs_q = qTc[0:64, n * 128:n * 128 + 128]
                for kb in range(5):
                    eb = c + kb
                    p.mm(psL[:], lhs_q, kTb[0:64, eb * 128:(eb + 1) * 128],
                         start=True, stop=True)
                    p.stt(lg[:, kb * 128:(kb + 1) * 128], psL[:], 0.125,
                          mks[:, kb * 128:(kb + 1) * 128], ALU.mult, ALU.add)
                p.rmax(mrow[:, 0:1], lg[:])
                p.ts(mrow[:, 1:2], mrow[:, 0:1], -1.0, None, ALU.mult)
                p.act(pb[:], lg[:], AF.Exp, bias=mrow[:, 1:2])
                p.rsum(mrow[:, 2:3], pb[:])
                p.recip(mrow[:, 3:4], mrow[:, 2:3])
                p.ts(lg[:], pb[:], mrow[:, 3:4], None, ALU.mult)
                for kb in range(5):
                    p.tp(ps_t[:], lg[:, kb * 128:(kb + 1) * 128], identt[:])
                    p.cp(pT[:, kb * 128:(kb + 1) * 128], ps_t[:])
                for kb in range(5):
                    eb = c + kb
                    p.mm(psO[:, 0:64], pT[:, kb * 128:(kb + 1) * 128],
                         vb[:, eb * 64:(eb + 1) * 64], start=(kb == 0), stop=(kb == 4))
                p.cp(at[:, n * 64:(n + 1) * 64], psO[:, 0:64])
            emit_tp8(p, at, s2, ps_t, identt[:])
            for h in range(2):
                for i in range(8):
                    p.mm(psF[:], s2[:, i * 128:(i + 1) * 128],
                         wf[:, i * 1024 + h * 512:i * 1024 + h * 512 + 512],
                         start=(i == 0), stop=False)
                p.mm(psF[:], ones1, brow[0:1, h * 512:h * 512 + 512], start=False, stop=True)
                p.cp(s3[:, h * 512:(h + 1) * 512], psF[:])
            p.tt(s4[:], s1[:], s3[:], ALU.add)
            p.dma(x3o[c * 128:(c + 1) * 128, :], s4[:])
        p.build_block(nc)
    return nc


def build_CD():
    nc = bass.Bass("TRN2", target_bir_lowering=False)
    p = Prog()
    EXT = 2560
    NH = 16
    HD = 64
    TOK = 2048
    QT = 512
    FH = 4096
    xe = nc.dram_tensor("xe", [EXT, D], F32, kind="ExternalInput").ap()
    wqT = nc.dram_tensor("wqT", [D, D], F32, kind="ExternalInput").ap()
    wkT = nc.dram_tensor("wkT", [D, HD], F32, kind="ExternalInput").ap()
    wvT = nc.dram_tensor("wvT", [D, HD], F32, kind="ExternalInput").ap()
    wfT = nc.dram_tensor("wfT", [D, D], F32, kind="ExternalInput").ap()
    bf = nc.dram_tensor("bf", [D], F32, kind="ExternalInput").ap()
    msk = nc.dram_tensor("msk", [16, 128, 640], F32, kind="ExternalInput").ap()
    cns = nc.dram_tensor("cns", [128, 128], F32, kind="ExternalInput").ap()
    rws = nc.dram_tensor("rws", [1, 128], F32, kind="ExternalInput").ap()
    wgT = nc.dram_tensor("wgT", [D, 8], F32, kind="ExternalInput").ap()
    W1 = nc.dram_tensor("W1", [8, D, FH], F32, kind="ExternalInput").ap()
    b1 = nc.dram_tensor("b1", [8, FH], F32, kind="ExternalInput").ap()
    W2 = nc.dram_tensor("W2", [8, FH, D], F32, kind="ExternalInput").ap()
    b2 = nc.dram_tensor("b2", [8, D], F32, kind="ExternalInput").ap()
    x3 = nc.dram_tensor("x3s", [TOK, D], F32, kind="Internal").ap()
    x4o = nc.dram_tensor("x4o", [TOK, D], F32, kind="ExternalOutput").ap()

    nch = QT // 128
    st = ExitStack()
    with st:
        EN = st.enter_context
        identt = EN(nc.sbuf_tensor("ident", [128, 128], F32))
        rows = EN(nc.sbuf_tensor("rows", [1, 128], F32))
        brow = EN(nc.sbuf_tensor("brow", [1, D], F32))
        wq = EN(nc.sbuf_tensor("wq", [128, 8192], F32))
        wf = EN(nc.sbuf_tensor("wf", [128, 8192], F32))
        wk = EN(nc.sbuf_tensor("wk", [128, 512], F32))
        wv = EN(nc.sbuf_tensor("wv", [128, 512], F32))
        kTb = EN(nc.sbuf_tensor("kT", [64, 2560], F32))
        vb = EN(nc.sbuf_tensor("vb", [128, 1280], F32))
        qTc = EN(nc.sbuf_tensor("qT", [64, 2048], F32))
        mks = EN(nc.sbuf_tensor("mk", [128, 640], F32))
        lg = EN(nc.sbuf_tensor("lg", [128, 640], F32))
        pb = EN(nc.sbuf_tensor("pb", [128, 640], F32))
        pT = EN(nc.sbuf_tensor("pT", [128, 640], F32))
        s1 = EN(nc.sbuf_tensor("s1", [128, D], F32))
        s2 = EN(nc.sbuf_tensor("s2", [128, D], F32))
        s3 = EN(nc.sbuf_tensor("s3", [128, D], F32))
        s4 = EN(nc.sbuf_tensor("s4", [128, D], F32))
        at = EN(nc.sbuf_tensor("at", [128, D], F32))
        mrow = EN(nc.sbuf_tensor("mrow", [128, 8], F32))
        wg = EN(nc.sbuf_tensor("wg", [128, 64], F32))
        b1r = EN(nc.sbuf_tensor("b1r", [1, 1024], F32))
        b2r = EN(nc.sbuf_tensor("b2r", [1, D], F32))
        lnTg = EN(nc.sbuf_tensor("lnTg", [128, QT * 8], F32))
        accg = EN(nc.sbuf_tensor("accg", [128, QT * 8], F32))
        yac = EN(nc.sbuf_tensor("yac", [128, QT * 8], F32))
        pscr = EN(nc.sbuf_tensor("pscr", [128, 40], F32))
        comb = EN(nc.sbuf_tensor("comb", [128, 8 * (QT // 128)], F32))
        ps_t = EN(nc.psum_tensor("ps_t", [128, 128], F32))
        psL = EN(nc.psum_tensor("psL", [128, 128], F32))
        psO = EN(nc.psum_tensor("psO", [128, 64], F32))
        psF = EN(nc.psum_tensor("psF", [128, 512], F32))
        psB = EN(nc.psum_tensor("psB", [128, 512], F32))
        psG = EN(nc.psum_tensor("psG", [128, 128], F32))
        psA = psF
        w1s, w2s = wq, wf
        p.dma(identt[:], cns)
        p.dma(rows[:], rws)
        ones1 = rows[0:1, 0:128]
        p.dma(brow[0:1, :], bf[None, :])
        for i in range(8):
            p.dma(wq[:, i * 1024:(i + 1) * 1024], wqT[i * 128:(i + 1) * 128, :])
            p.dma(wf[:, i * 1024:(i + 1) * 1024], wfT[i * 128:(i + 1) * 128, :])
            p.dma(wk[:, i * 64:(i + 1) * 64], wkT[i * 128:(i + 1) * 128, :])
            p.dma(wv[:, i * 64:(i + 1) * 64], wvT[i * 128:(i + 1) * 128, :])
        for ec in range(EXT // 128):
            p.dma(s1[:], xe[ec * 128:(ec + 1) * 128, :])
            emit_ln(p, s1, s2, s3, s4, mrow, mrow)
            emit_tp8(p, s4, s3, ps_t, identt[:])
            for i in range(8):
                p.mm(psL[0:64, 0:128], wk[:, i * 64:(i + 1) * 64],
                     s3[:, i * 128:(i + 1) * 128], start=(i == 0), stop=(i == 7))
            p.cp(kTb[0:64, ec * 128:(ec + 1) * 128], psL[0:64, 0:128])
            for i in range(8):
                p.mm(psO[:, 0:64], s3[:, i * 128:(i + 1) * 128],
                     wv[:, i * 64:(i + 1) * 64], start=(i == 0), stop=(i == 7))
            p.cp(vb[:, ec * 64:(ec + 1) * 64], psO[:, 0:64])
        for c in range(16):
            p.dma(mks[:], msk[c, :, :])
            p.dma(s1[:], xe[(c + 2) * 128:(c + 3) * 128, :])
            emit_ln(p, s1, s2, s3, s4, mrow, mrow)
            emit_tp8(p, s4, s3, ps_t, identt[:])
            for n in range(NH):
                for i in range(8):
                    p.mm(psL[0:64, 0:128], wq[:, i * 1024 + n * 64:i * 1024 + n * 64 + 64],
                         s3[:, i * 128:(i + 1) * 128], start=(i == 0), stop=(i == 7))
                p.cp(qTc[0:64, n * 128:(n + 1) * 128], psL[0:64, 0:128])
            for n in range(NH):
                lhs_q = qTc[0:64, n * 128:n * 128 + 128]
                for kb in range(5):
                    eb = c + kb
                    p.mm(psL[:], lhs_q, kTb[0:64, eb * 128:(eb + 1) * 128],
                         start=True, stop=True)
                    p.stt(lg[:, kb * 128:(kb + 1) * 128], psL[:], 0.125,
                          mks[:, kb * 128:(kb + 1) * 128], ALU.mult, ALU.add)
                p.rmax(mrow[:, 0:1], lg[:])
                p.ts(mrow[:, 1:2], mrow[:, 0:1], -1.0, None, ALU.mult)
                p.act(pb[:], lg[:], AF.Exp, bias=mrow[:, 1:2])
                p.rsum(mrow[:, 2:3], pb[:])
                p.recip(mrow[:, 3:4], mrow[:, 2:3])
                p.ts(lg[:], pb[:], mrow[:, 3:4], None, ALU.mult)
                for kb in range(5):
                    p.tp(ps_t[:], lg[:, kb * 128:(kb + 1) * 128], identt[:])
                    p.cp(pT[:, kb * 128:(kb + 1) * 128], ps_t[:])
                for kb in range(5):
                    eb = c + kb
                    p.mm(psO[:, 0:64], pT[:, kb * 128:(kb + 1) * 128],
                         vb[:, eb * 64:(eb + 1) * 64], start=(kb == 0), stop=(kb == 4))
                p.cp(at[:, n * 64:(n + 1) * 64], psO[:, 0:64])
            emit_tp8(p, at, s2, ps_t, identt[:])
            for h in range(2):
                for i in range(8):
                    p.mm(psF[:], s2[:, i * 128:(i + 1) * 128],
                         wf[:, i * 1024 + h * 512:i * 1024 + h * 512 + 512],
                         start=(i == 0), stop=False)
                p.mm(psF[:], ones1, brow[0:1, h * 512:h * 512 + 512], start=False, stop=True)
                p.cp(s3[:, h * 512:(h + 1) * 512], psF[:])
            p.tt(s4[:], s1[:], s3[:], ALU.add)
            p.dma(x3[c * 128:(c + 1) * 128, :], s4[:])
        for i in range(8):
            p.dma(wg[:, i * 8:(i + 1) * 8], wgT[i * 128:(i + 1) * 128, :])
        for g in range(TOK // QT):
            t0 = g * QT
            for c in range(nch):
                r0 = t0 + c * 128
                p.dma(s1[:], x3[r0:r0 + 128, :])
                emit_ln(p, s1, s2, s3, s4, mrow, mrow)
                emit_tp8(p, s4, lnTg[:, c * 1024:(c + 1) * 1024], ps_t, identt[:])
                for i in range(8):
                    p.mm(psG[:, 0:8], lnTg[:, c * 1024 + i * 128:c * 1024 + i * 128 + 128],
                         wg[:, i * 8:(i + 1) * 8], start=(i == 0), stop=(i == 7))
                emit_router(p, psG, pscr, comb[:, c * 8:(c + 1) * 8], mrow)
                p.memset(accg[:, c * 1024:(c + 1) * 1024], 0.0)
            for e in range(8):
                p.dma(b2r[0:1, :], b2[e:e + 1, :])
                for hq in range(4):
                    p.dma(b1r[0:1, :], b1[e:e + 1, hq * 1024:(hq + 1) * 1024])
                    for i in range(8):
                        p.dma(w1s[:, i * 1024:(i + 1) * 1024],
                              W1[e, i * 128:(i + 1) * 128, hq * 1024:(hq + 1) * 1024])
                    w2q = W2[e, hq * 1024:(hq + 1) * 1024, :]
                    for j in range(8):
                        p.dma(w2s[:, j * 1024:(j + 1) * 1024],
                              w2q[j * 128:(j + 1) * 128, :])
                    for c in range(nch):
                        lt = lnTg[:, c * 1024:(c + 1) * 1024]
                        for h in range(2):
                            ps = psA if h == 0 else psB
                            for i in range(8):
                                p.mm(ps[:], lt[:, i * 128:(i + 1) * 128],
                                     w1s[:, i * 1024 + h * 512:i * 1024 + h * 512 + 512],
                                     start=(i == 0), stop=False)
                            p.mm(ps[:], ones1,
                                 b1r[0:1, h * 512:h * 512 + 512],
                                 start=False, stop=True)
                            p.act(s1[:, h * 512:(h + 1) * 512], ps[:], AF.Gelu)
                        emit_tp8(p, s1, s2, ps_t, identt[:])
                        for h in range(2):
                            ps = psA if h == 0 else psB
                            for j in range(8):
                                p.mm(ps[:], s2[:, j * 128:(j + 1) * 128],
                                     w2s[:, j * 1024 + h * 512:j * 1024 + h * 512 + 512],
                                     start=(j == 0), stop=(hq == 3))
                            y = yac[:, c * 1024 + h * 512:c * 1024 + h * 512 + 512]
                            if hq == 0:
                                p.cp(y, ps[:])
                            elif hq < 3:
                                p.tt(y, y, ps[:], ALU.add)
                            else:
                                p.tt(y, y, ps[:], ALU.add)
                                a = accg[:, c * 1024 + h * 512:c * 1024 + h * 512 + 512]
                                p.mm(ps[:], ones1,
                                     b2r[0:1, h * 512:h * 512 + 512], start=True, stop=True)
                                p.tt(y, y, ps[:], ALU.add)
                                p.stt(a, y, comb[:, c * 8 + e:c * 8 + e + 1], a,
                                      ALU.mult, ALU.add)
            for c in range(nch):
                r0 = t0 + c * 128
                p.dma(s1[:], x3[r0:r0 + 128, :])
                p.tt(s2[:], s1[:], accg[:, c * 1024:(c + 1) * 1024], ALU.add)
                p.dma(x4o[r0:r0 + 128, :], s2[:])
        p.build_block(nc)
    return nc


def build_D():
    nc = bass.Bass("TRN2", target_bir_lowering=False)
    p = Prog()
    TOK = 2048
    QT = 512
    FH = 4096
    x3 = nc.dram_tensor("x3", [TOK, D], F32, kind="ExternalInput").ap()
    wgT = nc.dram_tensor("wgT", [D, 8], F32, kind="ExternalInput").ap()
    W1 = nc.dram_tensor("W1", [8, D, FH], F32, kind="ExternalInput").ap()
    b1 = nc.dram_tensor("b1", [8, FH], F32, kind="ExternalInput").ap()
    W2 = nc.dram_tensor("W2", [8, FH, D], F32, kind="ExternalInput").ap()
    b2 = nc.dram_tensor("b2", [8, D], F32, kind="ExternalInput").ap()
    cns = nc.dram_tensor("cns", [128, 128], F32, kind="ExternalInput").ap()
    rws = nc.dram_tensor("rws", [1, 128], F32, kind="ExternalInput").ap()
    x4o = nc.dram_tensor("x4o", [TOK, D], F32, kind="ExternalOutput").ap()

    nch = QT // 128
    st = ExitStack()
    with st:
        EN = st.enter_context
        identt = EN(nc.sbuf_tensor("ident", [128, 128], F32))
        rows = EN(nc.sbuf_tensor("rows", [1, 128], F32))
        wg = EN(nc.sbuf_tensor("wg", [128, 64], F32))
        w1s = EN(nc.sbuf_tensor("w1s", [128, 8192], F32))
        w2s = EN(nc.sbuf_tensor("w2s", [128, 8192], F32))
        b1r = EN(nc.sbuf_tensor("b1r", [1, FH], F32))
        b2r = EN(nc.sbuf_tensor("b2r", [1, D], F32))
        lnTg = EN(nc.sbuf_tensor("lnTg", [128, QT * 8], F32))
        accg = EN(nc.sbuf_tensor("accg", [128, QT * 8], F32))
        yac = EN(nc.sbuf_tensor("yac", [128, QT * 8], F32))
        s1 = EN(nc.sbuf_tensor("s1", [128, D], F32))
        s2 = EN(nc.sbuf_tensor("s2", [128, D], F32))
        s3 = EN(nc.sbuf_tensor("s3", [128, D], F32))
        s4 = EN(nc.sbuf_tensor("s4", [128, D], F32))
        mrow = EN(nc.sbuf_tensor("mrow", [128, 8], F32))
        pscr = EN(nc.sbuf_tensor("pscr", [128, 40], F32))
        comb = EN(nc.sbuf_tensor("comb", [128, 8 * (QT // 128)], F32))
        ps_t = EN(nc.psum_tensor("ps_t", [128, 128], F32))
        psA = EN(nc.psum_tensor("psA", [128, 512], F32))
        psB = EN(nc.psum_tensor("psB", [128, 512], F32))
        psG = EN(nc.psum_tensor("psG", [128, 128], F32))
        ones1 = rows[0:1, 0:128]
        p.dma(identt[:], cns)
        p.dma(rows[:], rws)
        for i in range(8):
            p.dma(wg[:, i * 8:(i + 1) * 8], wgT[i * 128:(i + 1) * 128, :])
        for g in range(TOK // QT):
            t0 = g * QT
            for c in range(nch):
                r0 = t0 + c * 128
                p.dma(s1[:], x3[r0:r0 + 128, :])
                emit_ln(p, s1, s2, s3, s4, mrow, mrow)
                emit_tp8(p, s4, lnTg[:, c * 1024:(c + 1) * 1024], ps_t, identt[:])
                for i in range(8):
                    p.mm(psG[:, 0:8], lnTg[:, c * 1024 + i * 128:c * 1024 + i * 128 + 128],
                         wg[:, i * 8:(i + 1) * 8], start=(i == 0), stop=(i == 7))
                emit_router(p, psG, pscr, comb[:, c * 8:(c + 1) * 8], mrow)
                p.memset(accg[:, c * 1024:(c + 1) * 1024], 0.0)
            for e in range(8):
                p.dma(b1r[0:1, :], b1[e:e + 1, :])
                p.dma(b2r[0:1, :], b2[e:e + 1, :])
                for hq in range(4):
                    for i in range(8):
                        p.dma(w1s[:, i * 1024:(i + 1) * 1024],
                              W1[e, i * 128:(i + 1) * 128, hq * 1024:(hq + 1) * 1024])
                    w2q = W2[e, hq * 1024:(hq + 1) * 1024, :]
                    for j in range(8):
                        p.dma(w2s[:, j * 1024:(j + 1) * 1024],
                              w2q[j * 128:(j + 1) * 128, :])
                    for c in range(nch):
                        lt = lnTg[:, c * 1024:(c + 1) * 1024]
                        for h in range(2):
                            ps = psA if h == 0 else psB
                            for i in range(8):
                                p.mm(ps[:], lt[:, i * 128:(i + 1) * 128],
                                     w1s[:, i * 1024 + h * 512:i * 1024 + h * 512 + 512],
                                     start=(i == 0), stop=False)
                            p.mm(ps[:], ones1,
                                 b1r[0:1, hq * 1024 + h * 512:hq * 1024 + h * 512 + 512],
                                 start=False, stop=True)
                            p.act(s1[:, h * 512:(h + 1) * 512], ps[:], AF.Gelu)
                        emit_tp8(p, s1, s2, ps_t, identt[:])
                        for h in range(2):
                            ps = psA if h == 0 else psB
                            for j in range(8):
                                p.mm(ps[:], s2[:, j * 128:(j + 1) * 128],
                                     w2s[:, j * 1024 + h * 512:j * 1024 + h * 512 + 512],
                                     start=(j == 0), stop=(hq == 3))
                            y = yac[:, c * 1024 + h * 512:c * 1024 + h * 512 + 512]
                            if hq == 0:
                                p.cp(y, ps[:])
                            elif hq < 3:
                                p.tt(y, y, ps[:], ALU.add)
                            else:
                                p.tt(y, y, ps[:], ALU.add)
                                a = accg[:, c * 1024 + h * 512:c * 1024 + h * 512 + 512]
                                p.mm(ps[:], ones1,
                                     b2r[0:1, h * 512:h * 512 + 512], start=True, stop=True)
                                p.tt(y, y, ps[:], ALU.add)
                                p.stt(a, y, comb[:, c * 8 + e:c * 8 + e + 1], a,
                                      ALU.mult, ALU.add)
            for c in range(nch):
                r0 = t0 + c * 128
                p.dma(s1[:], x3[r0:r0 + 128, :])
                p.tt(s2[:], s1[:], accg[:, c * 1024:(c + 1) * 1024], ALU.add)
                p.dma(x4o[r0:r0 + 128, :], s2[:])
        p.build_block(nc)
    return nc


_DBG = {}


def _run(nc, in_maps, names):
    res = run_bass_kernel_spmd(nc, in_maps, core_ids=list(range(NCORE)))
    return [[np.asarray(r[n]) for n in names] for r in res.results]


def kernel(**inp):
    f = np.float32
    x = np.ascontiguousarray(inp["L0_attn_x"], dtype=f)
    ident = np.eye(128, dtype=f)
    cumA = np.triu(np.ones((128, 128), f))
    shiftA = np.eye(128, k=1, dtype=f)
    rowsA = np.zeros((1, 256), f)
    rowsA[0, 0:128] = 1.0
    rowsA[0, 128] = 1.0
    rows1 = np.ones((1, 128), f)
    cnsA = np.concatenate([ident, cumA, shiftA], axis=1)

    wqT = np.ascontiguousarray(inp["L0_attn_Wq1"].T, f)
    wkT = np.ascontiguousarray(inp["L0_attn_Wk1"].T, f)
    wk2T = np.ascontiguousarray(inp["L0_attn_Wk2"].T, f)

    ncA = build_A()
    maps = []
    for c in range(NCORE):
        b, g = c // 4, c % 4
        sel = g * 256
        maps.append({
            "xb": np.ascontiguousarray(np.roll(x[b], -sel, axis=1)),
            "wq": np.ascontiguousarray(np.roll(wqT[:, sel:sel + 256], -sel, axis=0)),
            "wk": np.ascontiguousarray(np.roll(wkT[:, sel:sel + 256], -sel, axis=0)),
            "wk2": np.ascontiguousarray(np.roll(wk2T[:, sel:sel + 256], -sel, axis=0)),
            "bq": np.ascontiguousarray(inp["L0_attn_bq1"][sel:sel + 256], f),
            "bk": np.ascontiguousarray(inp["L0_attn_bk1"][sel:sel + 256], f),
            "bk2": np.ascontiguousarray(inp["L0_attn_bk2"][sel:sel + 256], f),
            "cns": cnsA, "rws": rowsA,
        })
    outA = _run(ncA, maps, ["aout"])
    A = np.zeros((BATCH, SEQ, D), f)
    for c in range(NCORE):
        b, g = c // 4, c % 4
        A[b][:, g * 256:(g + 1) * 256] = outA[c][0]
    _DBG["A"] = A

    xf = x.reshape(NTOK, D)
    Af = A.reshape(NTOK, D)
    ncB = build_B()
    maps = []
    shared = {
        "wprojT": np.ascontiguousarray(inp["L0_attn_Wproj"].T, f),
        "bproj": np.ascontiguousarray(inp["L0_attn_bproj"], f),
        "routw": np.ascontiguousarray(inp["L0_ffn_router_w"], f),
        "w1": np.ascontiguousarray(inp["L0_ffn_w1"], f),
        "w2": np.ascontiguousarray(inp["L0_ffn_w2"], f),
        "sgu": np.ascontiguousarray(inp["L0_ffn_shared_gate_up"], f),
        "sdn": np.ascontiguousarray(inp["L0_ffn_shared_down"], f),
        "sgw": np.ascontiguousarray(inp["L0_ffn_shared_gate_w"], f),
        "cns": ident, "rws": rows1,
    }
    for c in range(NCORE):
        m = dict(shared)
        m["xsh"] = np.ascontiguousarray(xf[c * 2048:(c + 1) * 2048])
        m["ash"] = np.ascontiguousarray(Af[c * 2048:(c + 1) * 2048])
        maps.append(m)
    outB = _run(ncB, maps, ["x2o"])
    x2 = np.concatenate([o[0] for o in outB], axis=0)
    _DBG["x2"] = x2

    x2r = x2.reshape(BATCH, SEQ, D)
    ncC = build_CD()
    maps = []
    sharedC = {
        "wqT": np.ascontiguousarray(inp["L1_attn_Wq"].T, f),
        "wkT": np.ascontiguousarray(inp["L1_attn_Wk"].T, f),
        "wvT": np.ascontiguousarray(inp["L1_attn_Wv"].T, f),
        "wfT": np.ascontiguousarray(inp["L1_attn_Wf"].T, f),
        "bf": np.ascontiguousarray(inp["L1_attn_bf"], f),
        "cns": ident, "rws": rows1,
        "wgT": np.ascontiguousarray(inp["L1_ffn_w_gate"].T, f),
        "W1": np.ascontiguousarray(inp["L1_ffn_W1"], f),
        "b1": np.ascontiguousarray(inp["L1_ffn_b1"], f),
        "W2": np.ascontiguousarray(inp["L1_ffn_W2"], f),
        "b2": np.ascontiguousarray(inp["L1_ffn_b2"], f),
    }
    for c in range(NCORE):
        b, gq = c // 4, c % 4
        qbase = gq * 2048
        ext = np.zeros((2560, D), f)
        lo = qbase - 256
        hi = qbase + 2048 + 256
        slo, shi = max(lo, 0), min(hi, SEQ)
        ext[slo - lo:shi - lo] = x2r[b, slo:shi]
        mask = np.full((16, 128, 640), -1e30, f)
        for cc in range(16):
            qpos = qbase + cc * 128 + np.arange(128)
            spos = lo + cc * 128 + np.arange(640)
            ok = (np.abs(qpos[:, None] - spos[None, :]) <= 256) & \
                 (spos[None, :] >= 0) & (spos[None, :] < SEQ)
            mask[cc][ok] = 0.0
        m = dict(sharedC)
        m["xe"] = ext
        m["msk"] = mask
        maps.append(m)
    outC = _run(ncC, maps, ["x4o"])
    x4 = np.concatenate([o[0] for o in outC], axis=0)
    return x4.reshape(BATCH, SEQ, D)


# version 03 (final)
# speedup vs baseline: 1.2217x; 1.2206x over previous
from contextlib import ExitStack
import numpy as np
import concourse.bass as bass
import concourse.mybir as mybir
from concourse.bass_utils import run_bass_kernel_spmd

F32 = mybir.dt.float32
AF = mybir.ActivationFunctionType
ALU = mybir.AluOpType
AX = mybir.AxisListType.X

D = 1024
NCORE = 8
SEQ = 8192
BATCH = 2
NTOK = BATCH * SEQ


class Prog:

    def __init__(self):
        self.ops = []
        self.c_tot = 0
        self.d_tot = 0

    def _emit(self, eng, method, *args, **kwargs):
        is_dma = method == "dma_start"
        self.ops.append((eng, method, args, kwargs, self.c_tot, self.d_tot, is_dma))
        if is_dma:
            self.d_tot += 16
        else:
            self.c_tot += 1

    def dma(self, dst, src, **kw):
        self._emit("sync", "dma_start", out=dst, in_=src, **kw)

    def mm(self, out, lhsT, rhs, start, stop):
        self._emit("tensor", "matmul", out, lhsT, rhs, start=start, stop=stop)

    def tp(self, out, in_, ident):
        self._emit("tensor", "transpose", out, in_, ident)

    def act(self, out, in_, func, bias=0.0, scale=1.0):
        self._emit("scalar", "activation", out, in_, func, bias=bias, scale=scale)

    def s_add(self, out, in_, c):
        self._emit("scalar", "add", out, in_, c)

    def s_mul(self, out, in_, c):
        self._emit("scalar", "mul", out, in_, c)

    def cp(self, out, in_):
        self._emit("scalar", "copy", out, in_)

    def tt(self, out, a, b, op):
        self._emit("vector", "tensor_tensor", out, a, b, op)

    def ts(self, out, in0, s1, s2, op0, op1=None):
        if s2 is None and op1 is None:
            self._emit("vector", "tensor_single_scalar", out, in0, s1, op0)
        else:
            self._emit("vector", "tensor_scalar", out, in0, s1, s2, op0, op1)

    def stt(self, out, in0, scalar, in1, op0, op1):
        self._emit("vector", "scalar_tensor_tensor", out, in0, scalar, in1, op0, op1)

    def rsum(self, out, in_):
        self._emit("vector", "reduce_sum", out, in_, AX)

    def rmax(self, out, in_):
        self._emit("vector", "reduce_max", out, in_, AX)

    def recip(self, out, in_):
        self._emit("vector", "reciprocal", out, in_)

    def memset(self, ap, v):
        self._emit("vector", "memset", ap, v)

    def build_block(self, nc):
        with (
            nc.semaphore("csem") as csem,
            nc.semaphore("dsem") as dsem,
            nc.Block() as block,
        ):
            skips = []
            prev_eng = None
            for (en, meth, args, kwargs, cw, dw, is_dma) in self.ops:
                skips.append(en == "tensor" and prev_eng == "tensor")
                prev_eng = en

            def mk(eng_name):
                def run(eng):
                    for op, skip in zip(self.ops, skips):
                        (en, meth, args, kwargs, cw, dw, is_dma) = op
                        if en != eng_name:
                            continue
                        if not skip:
                            if cw > 0:
                                eng.wait_ge(csem, cw)
                            if dw > 0:
                                eng.wait_ge(dsem, dw)
                        instr = getattr(eng, meth)(*args, **kwargs)
                        instr.then_inc(dsem if is_dma else csem, 16 if is_dma else 1)
                    if eng_name == "sync":
                        eng.wait_ge(csem, self.c_tot)
                        eng.wait_ge(dsem, self.d_tot)
                return run

            block.sync(mk("sync"))
            block.tensor(mk("tensor"))
            block.scalar(mk("scalar"))
            block.vector(mk("vector"))


def emit_ln(p, xin, xc, scr, ln, mrow, vrow, width=D):
    p.rsum(mrow[:, 0:1], xin[:, 0:width])
    p.ts(mrow[:, 1:2], mrow[:, 0:1], -1.0 / width, None, ALU.mult)
    p.ts(xc[:, 0:width], xin[:, 0:width], mrow[:, 1:2], None, ALU.add)
    p.act(scr[:, 0:width], xc[:, 0:width], AF.Square)
    p.rsum(vrow[:, 0:1], scr[:, 0:width])
    p.ts(vrow[:, 1:2], vrow[:, 0:1], 1.0 / width, 1e-5, ALU.mult, ALU.add)
    p.act(vrow[:, 2:3], vrow[:, 1:2], AF.Sqrt)
    p.recip(vrow[:, 3:4], vrow[:, 2:3])
    p.ts(ln[:, 0:width], xc[:, 0:width], vrow[:, 3:4], None, ALU.mult)


def emit_tp8(p, ln, lnT, ps_t, ident, nblk=8):
    for i in range(nblk):
        p.tp(ps_t[:, 0:128], ln[:, i * 128:(i + 1) * 128], ident)
        p.cp(lnT[:, i * 128:(i + 1) * 128], ps_t[:, 0:128])


def emit_router(p, psG, probs_scr, comb_out, mrow):
    ex = probs_scr[:, 0:8]
    pr = probs_scr[:, 8:16]
    eq = probs_scr[:, 16:24]
    pw = probs_scr[:, 24:32]
    mk2 = probs_scr[:, 32:40]
    p.rmax(mrow[:, 0:1], psG[:, 0:8])
    p.ts(mrow[:, 1:2], mrow[:, 0:1], -1.0, None, ALU.mult)
    p.act(ex, psG[:, 0:8], AF.Exp, bias=mrow[:, 1:2])
    p.rsum(mrow[:, 2:3], ex)
    p.recip(mrow[:, 3:4], mrow[:, 2:3])
    p.ts(pr, ex, mrow[:, 3:4], None, ALU.mult)
    p.rmax(mrow[:, 4:5], pr)
    p.ts(eq, pr, mrow[:, 4:5], None, ALU.is_ge)
    p.ts(pw, eq, -1.0, 1.0, ALU.mult, ALU.add)
    p.tt(pw, pr, pw, ALU.mult)
    p.rmax(mrow[:, 5:6], pw)
    p.ts(mk2, pr, mrow[:, 5:6], None, ALU.is_ge)
    p.tt(comb_out, pr, mk2, ALU.mult)


def build_A():
    nc = bass.Bass("TRN2", target_bir_lowering=False)
    p = Prog()
    xb = nc.dram_tensor("xb", [SEQ, D], F32, kind="ExternalInput").ap()
    wq = nc.dram_tensor("wq", [D, 256], F32, kind="ExternalInput").ap()
    wk = nc.dram_tensor("wk", [D, 256], F32, kind="ExternalInput").ap()
    wk2 = nc.dram_tensor("wk2", [D, 256], F32, kind="ExternalInput").ap()
    bq = nc.dram_tensor("bq", [256], F32, kind="ExternalInput").ap()
    bk = nc.dram_tensor("bk", [256], F32, kind="ExternalInput").ap()
    bk2 = nc.dram_tensor("bk2", [256], F32, kind="ExternalInput").ap()
    cns = nc.dram_tensor("cns", [128, 384], F32, kind="ExternalInput").ap()
    rws = nc.dram_tensor("rws", [1, 256], F32, kind="ExternalInput").ap()
    aout = nc.dram_tensor("aout", [SEQ, 256], F32, kind="ExternalOutput").ap()

    st = ExitStack()
    with st:
        EN = st.enter_context
        consts = EN(nc.sbuf_tensor("consts", [128, 384], F32))
        rows = EN(nc.sbuf_tensor("rows", [1, 256], F32))
        brow = EN(nc.sbuf_tensor("brow", [1, 768], F32))
        wqs = EN(nc.sbuf_tensor("wqs", [128, 2048], F32))
        wks = EN(nc.sbuf_tensor("wks", [128, 2048], F32))
        wk2s = EN(nc.sbuf_tensor("wk2s", [128, 2048], F32))
        xin = EN(nc.sbuf_tensor("xin", [128, D], F32))
        xc = EN(nc.sbuf_tensor("xc0", [128, D], F32))
        scr = EN(nc.sbuf_tensor("scr", [128, D], F32))
        ln = EN(nc.sbuf_tensor("ln", [128, D], F32))
        lnT = EN(nc.sbuf_tensor("lnT", [128, D], F32))
        mrow = EN(nc.sbuf_tensor("mrow", [128, 8], F32))
        Q = EN(nc.sbuf_tensor("Q", [128, 256], F32))
        K = EN(nc.sbuf_tensor("K", [128, 256], F32))
        T2 = EN(nc.sbuf_tensor("T2", [128, 256], F32))
        KV = EN(nc.sbuf_tensor("KV", [128, 256], F32))
        Kc = EN(nc.sbuf_tensor("Kc", [128, 256], F32))
        rKc = EN(nc.sbuf_tensor("rKc", [128, 256], F32))
        S = EN(nc.sbuf_tensor("S", [128, 256], F32))
        Oar = EN(nc.sbuf_tensor("Oar", [128, 256], F32))
        E = EN(nc.sbuf_tensor("E", [128, 256], F32))
        K2 = EN(nc.sbuf_tensor("K2", [128, 256], F32))
        K2E = EN(nc.sbuf_tensor("K2E", [128, 256], F32))
        S2 = EN(nc.sbuf_tensor("S2", [128, 256], F32))
        Oma = EN(nc.sbuf_tensor("Oma", [128, 256], F32))
        Ao = EN(nc.sbuf_tensor("Ao", [128, 256], F32))
        carK = EN(nc.sbuf_tensor("carK", [1, 256], F32))
        carKV = EN(nc.sbuf_tensor("carKV", [1, 256], F32))
        carS2 = EN(nc.sbuf_tensor("carS2", [1, 256], F32))
        pOar = EN(nc.sbuf_tensor("pOar", [1, 256], F32))
        pT2 = EN(nc.sbuf_tensor("pT2", [1, 256], F32))
        pQ = EN(nc.sbuf_tensor("pQ", [1, 256], F32))
        cext = EN(nc.sbuf_tensor("cext", [128, 256], F32))
        ps_t = EN(nc.psum_tensor("ps_t", [128, 128], F32))
        psX = EN(nc.psum_tensor("psX", [128, 256], F32))
        ident = consts[:, 0:128]
        cumA = consts[:, 128:256]
        shiftA = consts[:, 256:384]
        ones1 = rows[0:1, 0:128]
        sel0 = rows[0:1, 128:256]
        p.dma(consts[:], cns)
        p.dma(rows[:], rws)
        p.dma(brow[0:1, 0:256], bq[None, :])
        p.dma(brow[0:1, 256:512], bk[None, :])
        p.dma(brow[0:1, 512:768], bk2[None, :])
        for i in range(8):
            p.dma(wqs[:, i * 256:(i + 1) * 256], wq[i * 128:(i + 1) * 128, :])
            p.dma(wks[:, i * 256:(i + 1) * 256], wk[i * 128:(i + 1) * 128, :])
            p.dma(wk2s[:, i * 256:(i + 1) * 256], wk2[i * 128:(i + 1) * 128, :])
        for t in (carK, carKV, carS2, pOar, pQ):
            p.memset(t[0:1, :], 0.0)
        p.ts(cext[0:1, 0:256], brow[0:1, 512:768], 6.25e-4, None, ALU.mult)
        p.act(pT2[0:1, :], cext[0:1, 0:256], AF.Sigmoid)

        for c in range(SEQ // 128):
            p.dma(xin[:], xb[c * 128:(c + 1) * 128, :])
            emit_ln(p, xin, xc, scr, ln, mrow, mrow)
            emit_tp8(p, ln, lnT, ps_t, ident)
            V = ln[:, 0:256]
            for (ws, bo, dst, fn, sc) in (
                (wqs, 0, Q, AF.Sigmoid, None),
                (wks, 256, K, AF.Exp, None),
                (wk2s, 512, T2, AF.Sigmoid, 6.25e-4),
            ):
                for i in range(8):
                    p.mm(psX[:], lnT[:, i * 128:(i + 1) * 128],
                         ws[:, i * 256:(i + 1) * 256], start=(i == 0), stop=False)
                p.mm(psX[:], ones1, brow[0:1, bo:bo + 256], start=False, stop=True)
                if sc is None:
                    p.act(dst[:], psX[:], fn)
                else:
                    p.ts(scr[:, 0:256], psX[:], sc, None, ALU.mult)
                    p.act(dst[:], scr[:, 0:256], fn)
            p.tt(KV[:], K[:], V, ALU.mult)
            p.mm(psX[:], cumA, K[:], start=True, stop=False)
            p.mm(psX[:], ones1, carK[0:1, :], start=False, stop=True)
            p.cp(cext[:], psX[:])
            p.dma(carK[0:1, :], cext[127:128, :])
            p.ts(Kc[:], psX[:], 1e-6, None, ALU.add)
            p.recip(rKc[:], Kc[:])
            p.mm(psX[:], cumA, KV[:], start=True, stop=False)
            p.mm(psX[:], ones1, carKV[0:1, :], start=False, stop=True)
            p.cp(cext[:], psX[:])
            p.dma(carKV[0:1, :], cext[127:128, :])
            p.tt(S[:], psX[:], rKc[:], ALU.mult)
            p.tt(Oar[:], S[:], Q[:], ALU.mult)
            p.mm(psX[:], shiftA, Oar[:], start=True, stop=False)
            p.mm(psX[:], sel0, pOar[0:1, :], start=False, stop=True)
            p.tt(E[:], V, psX[:], ALU.subtract)
            if c == 0:
                p.memset(E[0:1, :], 0.0)
            p.dma(pOar[0:1, :], Oar[127:128, :])
            p.mm(psX[:], shiftA, T2[:], start=True, stop=False)
            p.mm(psX[:], sel0, pT2[0:1, :], start=False, stop=True)
            p.cp(K2[:], psX[:])
            p.dma(pT2[0:1, :], T2[127:128, :])
            p.tt(K2E[:], K2[:], E[:], ALU.mult)
            p.mm(psX[:], cumA, K2E[:], start=True, stop=False)
            p.mm(psX[:], ones1, carS2[0:1, :], start=False, stop=True)
            p.cp(S2[:], psX[:])
            p.dma(carS2[0:1, :], S2[127:128, :])
            p.mm(psX[:], shiftA, Q[:], start=True, stop=False)
            p.mm(psX[:], sel0, pQ[0:1, :], start=False, stop=True)
            p.stt(Oma[:], psX[:], 6.25e-4, S2[:], ALU.mult, ALU.mult)
            p.dma(pQ[0:1, :], Q[127:128, :])
            p.tt(Ao[:], Oar[:], Oma[:], ALU.add)
            p.dma(aout[c * 128:(c + 1) * 128, :], Ao[:])
        p.build_block(nc)
    return nc


def build_B():
    nc = bass.Bass("TRN2", target_bir_lowering=False)
    p = Prog()
    TOK = 2048
    QT = 512
    xsh = nc.dram_tensor("xsh", [TOK, D], F32, kind="ExternalInput").ap()
    ash = nc.dram_tensor("ash", [TOK, D], F32, kind="ExternalInput").ap()
    wprojT = nc.dram_tensor("wprojT", [D, D], F32, kind="ExternalInput").ap()
    bproj = nc.dram_tensor("bproj", [D], F32, kind="ExternalInput").ap()
    routw = nc.dram_tensor("routw", [D, 8], F32, kind="ExternalInput").ap()
    w1 = nc.dram_tensor("w1", [8, D, D], F32, kind="ExternalInput").ap()
    w2 = nc.dram_tensor("w2", [8, D, D], F32, kind="ExternalInput").ap()
    sgu = nc.dram_tensor("sgu", [D, 2048], F32, kind="ExternalInput").ap()
    sdn = nc.dram_tensor("sdn", [D, D], F32, kind="ExternalInput").ap()
    sgw = nc.dram_tensor("sgw", [D, 1], F32, kind="ExternalInput").ap()
    cns = nc.dram_tensor("cns", [128, 128], F32, kind="ExternalInput").ap()
    rws = nc.dram_tensor("rws", [1, 128], F32, kind="ExternalInput").ap()
    x1scr = nc.dram_tensor("x1scr", [TOK, D], F32, kind="Internal").ap()
    x2o = nc.dram_tensor("x2o", [TOK, D], F32, kind="ExternalOutput").ap()

    nch = QT // 128
    st = ExitStack()
    with st:
        EN = st.enter_context
        identt = EN(nc.sbuf_tensor("ident", [128, 128], F32))
        rows = EN(nc.sbuf_tensor("rows", [1, 128], F32))
        brow = EN(nc.sbuf_tensor("brow", [1, D], F32))
        big = EN(nc.sbuf_tensor("big", [128, 16384], F32))
        wdn = EN(nc.sbuf_tensor("wdn", [128, 8192], F32))
        rw = EN(nc.sbuf_tensor("rw", [128, 64], F32))
        sgws = EN(nc.sbuf_tensor("sgws", [128, 8], F32))
        lnTg = EN(nc.sbuf_tensor("lnTg", [128, 4096], F32))
        accg = EN(nc.sbuf_tensor("accg", [128, 4096], F32))
        s1 = EN(nc.sbuf_tensor("s1", [128, D], F32))
        s2 = EN(nc.sbuf_tensor("s2", [128, D], F32))
        s3 = EN(nc.sbuf_tensor("s3", [128, D], F32))
        s4 = EN(nc.sbuf_tensor("s4", [128, D], F32))
        s5 = EN(nc.sbuf_tensor("s5", [128, D], F32))
        s6 = EN(nc.sbuf_tensor("s6", [128, D], F32))
        mrow = EN(nc.sbuf_tensor("mrow", [128, 8], F32))
        pscr = EN(nc.sbuf_tensor("pscr", [128, 40], F32))
        comb = EN(nc.sbuf_tensor("comb", [128, 32], F32))
        ps_t = EN(nc.psum_tensor("ps_t", [128, 128], F32))
        psA = EN(nc.psum_tensor("psA", [128, 512], F32))
        psB = EN(nc.psum_tensor("psB", [128, 512], F32))
        psG = EN(nc.psum_tensor("psG", [128, 128], F32))
        ones1 = rows[0:1, 0:128]
        p.dma(identt[:], cns)
        p.dma(rows[:], rws)
        p.dma(brow[0:1, :], bproj[None, :])
        for i in range(8):
            p.dma(rw[:, i * 8:(i + 1) * 8], routw[i * 128:(i + 1) * 128, :])
            p.dma(sgws[:, i:i + 1], sgw[i * 128:(i + 1) * 128, :])
        for g in range(TOK // QT):
            t0 = g * QT
            for i in range(8):
                p.dma(big[:, i * 1024:(i + 1) * 1024], wprojT[i * 128:(i + 1) * 128, :])
            for c in range(nch):
                r0 = t0 + c * 128
                p.dma(s1[:], xsh[r0:r0 + 128, :])
                p.dma(s2[:], ash[r0:r0 + 128, :])
                emit_tp8(p, s2, s3, ps_t, identt[:])
                for h in range(2):
                    ps = psA if h == 0 else psB
                    for i in range(8):
                        p.mm(ps[:], s3[:, i * 128:(i + 1) * 128],
                             big[:, i * 1024 + h * 512: i * 1024 + h * 512 + 512],
                             start=(i == 0), stop=False)
                    p.mm(ps[:], ones1, brow[0:1, h * 512:h * 512 + 512],
                         start=False, stop=True)
                    p.tt(s4[:, h * 512:h * 512 + 512], s1[:, h * 512:h * 512 + 512],
                         ps[:], ALU.add)
                p.dma(x1scr[r0:r0 + 128, :], s4[:])
                emit_ln(p, s4, s2, s5, s6, mrow, mrow)
                emit_tp8(p, s6, lnTg[:, c * 1024:(c + 1) * 1024], ps_t, identt[:])
                for i in range(8):
                    p.mm(psG[:, 0:8], lnTg[:, c * 1024 + i * 128:c * 1024 + i * 128 + 128],
                         rw[:, i * 8:(i + 1) * 8], start=(i == 0), stop=(i == 7))
                emit_router(p, psG, pscr, comb[:, c * 8:(c + 1) * 8], mrow)
            for i in range(8):
                p.dma(big[:, i * 2048:(i + 1) * 2048], sgu[i * 128:(i + 1) * 128, :])
                p.dma(wdn[:, i * 1024:(i + 1) * 1024], sdn[i * 128:(i + 1) * 128, :])
            for c in range(nch):
                lt = lnTg[:, c * 1024:(c + 1) * 1024]
                for q in range(4):
                    for i in range(8):
                        p.mm(psA[:], lt[:, i * 128:(i + 1) * 128],
                             big[:, i * 2048 + q * 512: i * 2048 + q * 512 + 512],
                             start=(i == 0), stop=(i == 7))
                    if q < 2:
                        p.act(s1[:, q * 512:(q + 1) * 512], psA[:], AF.Silu)
                    else:
                        p.cp(s2[:, (q - 2) * 512:(q - 1) * 512], psA[:])
                p.tt(s3[:], s1[:], s2[:], ALU.mult)
                emit_tp8(p, s3, s4, ps_t, identt[:])
                for i in range(8):
                    p.mm(psG[:, 0:1], lt[:, i * 128:(i + 1) * 128], sgws[:, i:i + 1],
                         start=(i == 0), stop=(i == 7))
                p.act(mrow[:, 6:7], psG[:, 0:1], AF.Sigmoid)
                for h in range(2):
                    ps = psA if h == 0 else psB
                    for i in range(8):
                        p.mm(ps[:], s4[:, i * 128:(i + 1) * 128],
                             wdn[:, i * 1024 + h * 512:i * 1024 + h * 512 + 512],
                             start=(i == 0), stop=(i == 7))
                    p.ts(accg[:, c * 1024 + h * 512:c * 1024 + h * 512 + 512],
                         ps[:], mrow[:, 6:7], None, ALU.mult)
            for e in range(8):
                for i in range(8):
                    p.dma(big[:, i * 1024:(i + 1) * 1024], w1[e, i * 128:(i + 1) * 128, :])
                    p.dma(big[:, 8192 + i * 1024:8192 + (i + 1) * 1024],
                          w2[e, i * 128:(i + 1) * 128, :])
                for c in range(nch):
                    lt = lnTg[:, c * 1024:(c + 1) * 1024]
                    for h in range(2):
                        ps = psA if h == 0 else psB
                        for i in range(8):
                            p.mm(ps[:], lt[:, i * 128:(i + 1) * 128],
                                 big[:, i * 1024 + h * 512:i * 1024 + h * 512 + 512],
                                 start=(i == 0), stop=(i == 7))
                        p.act(s1[:, h * 512:(h + 1) * 512], ps[:], AF.Silu)
                    emit_tp8(p, s1, s2, ps_t, identt[:])
                    for h in range(2):
                        ps = psA if h == 0 else psB
                        for i in range(8):
                            p.mm(ps[:], s2[:, i * 128:(i + 1) * 128],
                                 big[:, 8192 + i * 1024 + h * 512:8192 + i * 1024 + h * 512 + 512],
                                 start=(i == 0), stop=(i == 7))
                        a = accg[:, c * 1024 + h * 512:c * 1024 + h * 512 + 512]
                        p.stt(a, ps[:], comb[:, c * 8 + e:c * 8 + e + 1], a,
                              ALU.mult, ALU.add)
            for c in range(nch):
                r0 = t0 + c * 128
                p.dma(s1[:], x1scr[r0:r0 + 128, :])
                p.tt(s2[:], s1[:], accg[:, c * 1024:(c + 1) * 1024], ALU.add)
                p.dma(x2o[r0:r0 + 128, :], s2[:])
        p.build_block(nc)
    return nc


def build_C():
    nc = bass.Bass("TRN2", target_bir_lowering=False)
    p = Prog()
    EXT = 2560
    NH = 16
    HD = 64
    xe = nc.dram_tensor("xe", [EXT, D], F32, kind="ExternalInput").ap()
    wqT = nc.dram_tensor("wqT", [D, D], F32, kind="ExternalInput").ap()
    wkT = nc.dram_tensor("wkT", [D, HD], F32, kind="ExternalInput").ap()
    wvT = nc.dram_tensor("wvT", [D, HD], F32, kind="ExternalInput").ap()
    wfT = nc.dram_tensor("wfT", [D, D], F32, kind="ExternalInput").ap()
    bf = nc.dram_tensor("bf", [D], F32, kind="ExternalInput").ap()
    msk = nc.dram_tensor("msk", [16, 128, 640], F32, kind="ExternalInput").ap()
    cns = nc.dram_tensor("cns", [128, 128], F32, kind="ExternalInput").ap()
    rws = nc.dram_tensor("rws", [1, 128], F32, kind="ExternalInput").ap()
    x3o = nc.dram_tensor("x3o", [2048, D], F32, kind="ExternalOutput").ap()

    st = ExitStack()
    with st:
        EN = st.enter_context
        identt = EN(nc.sbuf_tensor("ident", [128, 128], F32))
        rows = EN(nc.sbuf_tensor("rows", [1, 128], F32))
        brow = EN(nc.sbuf_tensor("brow", [1, D], F32))
        wq = EN(nc.sbuf_tensor("wq", [128, 8192], F32))
        wf = EN(nc.sbuf_tensor("wf", [128, 8192], F32))
        wk = EN(nc.sbuf_tensor("wk", [128, 512], F32))
        wv = EN(nc.sbuf_tensor("wv", [128, 512], F32))
        kTb = EN(nc.sbuf_tensor("kT", [64, 2560], F32))
        vb = EN(nc.sbuf_tensor("vb", [128, 1280], F32))
        qTc = EN(nc.sbuf_tensor("qT", [64, 2048], F32))
        mks = EN(nc.sbuf_tensor("mk", [128, 640], F32))
        lg = EN(nc.sbuf_tensor("lg", [128, 640], F32))
        pb = EN(nc.sbuf_tensor("pb", [128, 640], F32))
        pT = EN(nc.sbuf_tensor("pT", [128, 640], F32))
        s1 = EN(nc.sbuf_tensor("s1", [128, D], F32))
        s2 = EN(nc.sbuf_tensor("s2", [128, D], F32))
        s3 = EN(nc.sbuf_tensor("s3", [128, D], F32))
        s4 = EN(nc.sbuf_tensor("s4", [128, D], F32))
        at = EN(nc.sbuf_tensor("at", [128, D], F32))
        mrow = EN(nc.sbuf_tensor("mrow", [128, 8], F32))
        ps_t = EN(nc.psum_tensor("ps_t", [128, 128], F32))
        psL = EN(nc.psum_tensor("psL", [128, 128], F32))
        psO = EN(nc.psum_tensor("psO", [128, 64], F32))
        psF = EN(nc.psum_tensor("psF", [128, 512], F32))
        p.dma(identt[:], cns)
        p.dma(rows[:], rws)
        ones1 = rows[0:1, 0:128]
        p.dma(brow[0:1, :], bf[None, :])
        for i in range(8):
            p.dma(wq[:, i * 1024:(i + 1) * 1024], wqT[i * 128:(i + 1) * 128, :])
            p.dma(wf[:, i * 1024:(i + 1) * 1024], wfT[i * 128:(i + 1) * 128, :])
            p.dma(wk[:, i * 64:(i + 1) * 64], wkT[i * 128:(i + 1) * 128, :])
            p.dma(wv[:, i * 64:(i + 1) * 64], wvT[i * 128:(i + 1) * 128, :])
        for ec in range(EXT // 128):
            p.dma(s1[:], xe[ec * 128:(ec + 1) * 128, :])
            emit_ln(p, s1, s2, s3, s4, mrow, mrow)
            emit_tp8(p, s4, s3, ps_t, identt[:])
            for i in range(8):
                p.mm(psL[0:64, 0:128], wk[:, i * 64:(i + 1) * 64],
                     s3[:, i * 128:(i + 1) * 128], start=(i == 0), stop=(i == 7))
            p.cp(kTb[0:64, ec * 128:(ec + 1) * 128], psL[0:64, 0:128])
            for i in range(8):
                p.mm(psO[:, 0:64], s3[:, i * 128:(i + 1) * 128],
                     wv[:, i * 64:(i + 1) * 64], start=(i == 0), stop=(i == 7))
            p.cp(vb[:, ec * 64:(ec + 1) * 64], psO[:, 0:64])
        for c in range(16):
            p.dma(mks[:], msk[c, :, :])
            p.dma(s1[:], xe[(c + 2) * 128:(c + 3) * 128, :])
            emit_ln(p, s1, s2, s3, s4, mrow, mrow)
            emit_tp8(p, s4, s3, ps_t, identt[:])
            for n in range(NH):
                for i in range(8):
                    p.mm(psL[0:64, 0:128], wq[:, i * 1024 + n * 64:i * 1024 + n * 64 + 64],
                         s3[:, i * 128:(i + 1) * 128], start=(i == 0), stop=(i == 7))
                p.cp(qTc[0:64, n * 128:(n + 1) * 128], psL[0:64, 0:128])
            for n in range(NH):
                lhs_q = qTc[0:64, n * 128:n * 128 + 128]
                for kb in range(5):
                    eb = c + kb
                    p.mm(psL[:], lhs_q, kTb[0:64, eb * 128:(eb + 1) * 128],
                         start=True, stop=True)
                    p.stt(lg[:, kb * 128:(kb + 1) * 128], psL[:], 0.125,
                          mks[:, kb * 128:(kb + 1) * 128], ALU.mult, ALU.add)
                p.rmax(mrow[:, 0:1], lg[:])
                p.ts(mrow[:, 1:2], mrow[:, 0:1], -1.0, None, ALU.mult)
                p.act(pb[:], lg[:], AF.Exp, bias=mrow[:, 1:2])
                p.rsum(mrow[:, 2:3], pb[:])
                p.recip(mrow[:, 3:4], mrow[:, 2:3])
                p.ts(lg[:], pb[:], mrow[:, 3:4], None, ALU.mult)
                for kb in range(5):
                    p.tp(ps_t[:], lg[:, kb * 128:(kb + 1) * 128], identt[:])
                    p.cp(pT[:, kb * 128:(kb + 1) * 128], ps_t[:])
                for kb in range(5):
                    eb = c + kb
                    p.mm(psO[:, 0:64], pT[:, kb * 128:(kb + 1) * 128],
                         vb[:, eb * 64:(eb + 1) * 64], start=(kb == 0), stop=(kb == 4))
                p.cp(at[:, n * 64:(n + 1) * 64], psO[:, 0:64])
            emit_tp8(p, at, s2, ps_t, identt[:])
            for h in range(2):
                for i in range(8):
                    p.mm(psF[:], s2[:, i * 128:(i + 1) * 128],
                         wf[:, i * 1024 + h * 512:i * 1024 + h * 512 + 512],
                         start=(i == 0), stop=False)
                p.mm(psF[:], ones1, brow[0:1, h * 512:h * 512 + 512], start=False, stop=True)
                p.cp(s3[:, h * 512:(h + 1) * 512], psF[:])
            p.tt(s4[:], s1[:], s3[:], ALU.add)
            p.dma(x3o[c * 128:(c + 1) * 128, :], s4[:])
        p.build_block(nc)
    return nc


def build_CD():
    nc = bass.Bass("TRN2", target_bir_lowering=False)
    p = Prog()
    EXT = 2560
    NH = 16
    HD = 64
    TOK = 2048
    QT = 512
    FH = 4096
    xe = nc.dram_tensor("xe", [EXT, D], F32, kind="ExternalInput").ap()
    wqT = nc.dram_tensor("wqT", [D, D], F32, kind="ExternalInput").ap()
    wkT = nc.dram_tensor("wkT", [D, HD], F32, kind="ExternalInput").ap()
    wvT = nc.dram_tensor("wvT", [D, HD], F32, kind="ExternalInput").ap()
    wfT = nc.dram_tensor("wfT", [D, D], F32, kind="ExternalInput").ap()
    bf = nc.dram_tensor("bf", [D], F32, kind="ExternalInput").ap()
    msk = nc.dram_tensor("msk", [16, 128, 640], F32, kind="ExternalInput").ap()
    cns = nc.dram_tensor("cns", [128, 128], F32, kind="ExternalInput").ap()
    rws = nc.dram_tensor("rws", [1, 128], F32, kind="ExternalInput").ap()
    wgT = nc.dram_tensor("wgT", [D, 8], F32, kind="ExternalInput").ap()
    W1 = nc.dram_tensor("W1", [8, D, FH], F32, kind="ExternalInput").ap()
    b1 = nc.dram_tensor("b1", [8, FH], F32, kind="ExternalInput").ap()
    W2 = nc.dram_tensor("W2", [8, FH, D], F32, kind="ExternalInput").ap()
    b2 = nc.dram_tensor("b2", [8, D], F32, kind="ExternalInput").ap()
    x3 = nc.dram_tensor("x3s", [TOK, D], F32, kind="Internal").ap()
    x4o = nc.dram_tensor("x4o", [TOK, D], F32, kind="ExternalOutput").ap()

    nch = QT // 128
    st = ExitStack()
    with st:
        EN = st.enter_context
        identt = EN(nc.sbuf_tensor("ident", [128, 128], F32))
        rows = EN(nc.sbuf_tensor("rows", [1, 128], F32))
        brow = EN(nc.sbuf_tensor("brow", [1, D], F32))
        wq = EN(nc.sbuf_tensor("wq", [128, 8192], F32))
        wf = EN(nc.sbuf_tensor("wf", [128, 8192], F32))
        wk = EN(nc.sbuf_tensor("wk", [128, 512], F32))
        wv = EN(nc.sbuf_tensor("wv", [128, 512], F32))
        kTb = EN(nc.sbuf_tensor("kT", [64, 2560], F32))
        vb = EN(nc.sbuf_tensor("vb", [128, 1280], F32))
        qTc = EN(nc.sbuf_tensor("qT", [64, 2048], F32))
        mks = EN(nc.sbuf_tensor("mk", [128, 640], F32))
        lg = EN(nc.sbuf_tensor("lg", [128, 640], F32))
        pb = EN(nc.sbuf_tensor("pb", [128, 640], F32))
        pT = EN(nc.sbuf_tensor("pT", [128, 640], F32))
        s1 = EN(nc.sbuf_tensor("s1", [128, D], F32))
        s2 = EN(nc.sbuf_tensor("s2", [128, D], F32))
        s3 = EN(nc.sbuf_tensor("s3", [128, D], F32))
        s4 = EN(nc.sbuf_tensor("s4", [128, D], F32))
        at = EN(nc.sbuf_tensor("at", [128, D], F32))
        mrow = EN(nc.sbuf_tensor("mrow", [128, 8], F32))
        wg = EN(nc.sbuf_tensor("wg", [128, 64], F32))
        b1r = EN(nc.sbuf_tensor("b1r", [1, 1024], F32))
        b2r = EN(nc.sbuf_tensor("b2r", [1, D], F32))
        lnTg = EN(nc.sbuf_tensor("lnTg", [128, QT * 8], F32))
        accg = EN(nc.sbuf_tensor("accg", [128, QT * 8], F32))
        yac = EN(nc.sbuf_tensor("yac", [128, QT * 8], F32))
        pscr = EN(nc.sbuf_tensor("pscr", [128, 40], F32))
        comb = EN(nc.sbuf_tensor("comb", [128, 8 * (QT // 128)], F32))
        ps_t = EN(nc.psum_tensor("ps_t", [128, 128], F32))
        psL = EN(nc.psum_tensor("psL", [128, 128], F32))
        psO = EN(nc.psum_tensor("psO", [128, 64], F32))
        psF = EN(nc.psum_tensor("psF", [128, 512], F32))
        psB = EN(nc.psum_tensor("psB", [128, 512], F32))
        psG = EN(nc.psum_tensor("psG", [128, 128], F32))
        psA = psF
        w1s, w2s = wq, wf
        p.dma(identt[:], cns)
        p.dma(rows[:], rws)
        ones1 = rows[0:1, 0:128]
        p.dma(brow[0:1, :], bf[None, :])
        for i in range(8):
            p.dma(wq[:, i * 1024:(i + 1) * 1024], wqT[i * 128:(i + 1) * 128, :])
            p.dma(wf[:, i * 1024:(i + 1) * 1024], wfT[i * 128:(i + 1) * 128, :])
            p.dma(wk[:, i * 64:(i + 1) * 64], wkT[i * 128:(i + 1) * 128, :])
            p.dma(wv[:, i * 64:(i + 1) * 64], wvT[i * 128:(i + 1) * 128, :])
        for ec in range(EXT // 128):
            p.dma(s1[:], xe[ec * 128:(ec + 1) * 128, :])
            emit_ln(p, s1, s2, s3, s4, mrow, mrow)
            emit_tp8(p, s4, s3, ps_t, identt[:])
            for i in range(8):
                p.mm(psL[0:64, 0:128], wk[:, i * 64:(i + 1) * 64],
                     s3[:, i * 128:(i + 1) * 128], start=(i == 0), stop=(i == 7))
            p.cp(kTb[0:64, ec * 128:(ec + 1) * 128], psL[0:64, 0:128])
            for i in range(8):
                p.mm(psO[:, 0:64], s3[:, i * 128:(i + 1) * 128],
                     wv[:, i * 64:(i + 1) * 64], start=(i == 0), stop=(i == 7))
            p.cp(vb[:, ec * 64:(ec + 1) * 64], psO[:, 0:64])
        for c in range(16):
            p.dma(mks[:], msk[c, :, :])
            p.dma(s1[:], xe[(c + 2) * 128:(c + 3) * 128, :])
            emit_ln(p, s1, s2, s3, s4, mrow, mrow)
            emit_tp8(p, s4, s3, ps_t, identt[:])
            for n in range(NH):
                for i in range(8):
                    p.mm(psL[0:64, 0:128], wq[:, i * 1024 + n * 64:i * 1024 + n * 64 + 64],
                         s3[:, i * 128:(i + 1) * 128], start=(i == 0), stop=(i == 7))
                p.cp(qTc[0:64, n * 128:(n + 1) * 128], psL[0:64, 0:128])
            for n in range(NH):
                lhs_q = qTc[0:64, n * 128:n * 128 + 128]
                for kb in range(5):
                    eb = c + kb
                    p.mm(psL[:], lhs_q, kTb[0:64, eb * 128:(eb + 1) * 128],
                         start=True, stop=True)
                    p.stt(lg[:, kb * 128:(kb + 1) * 128], psL[:], 0.125,
                          mks[:, kb * 128:(kb + 1) * 128], ALU.mult, ALU.add)
                p.rmax(mrow[:, 0:1], lg[:])
                p.ts(mrow[:, 1:2], mrow[:, 0:1], -1.0, None, ALU.mult)
                p.act(pb[:], lg[:], AF.Exp, bias=mrow[:, 1:2])
                p.rsum(mrow[:, 2:3], pb[:])
                p.recip(mrow[:, 3:4], mrow[:, 2:3])
                p.ts(lg[:], pb[:], mrow[:, 3:4], None, ALU.mult)
                for kb in range(5):
                    p.tp(ps_t[:], lg[:, kb * 128:(kb + 1) * 128], identt[:])
                    p.cp(pT[:, kb * 128:(kb + 1) * 128], ps_t[:])
                for kb in range(5):
                    eb = c + kb
                    p.mm(psO[:, 0:64], pT[:, kb * 128:(kb + 1) * 128],
                         vb[:, eb * 64:(eb + 1) * 64], start=(kb == 0), stop=(kb == 4))
                p.cp(at[:, n * 64:(n + 1) * 64], psO[:, 0:64])
            emit_tp8(p, at, s2, ps_t, identt[:])
            for h in range(2):
                for i in range(8):
                    p.mm(psF[:], s2[:, i * 128:(i + 1) * 128],
                         wf[:, i * 1024 + h * 512:i * 1024 + h * 512 + 512],
                         start=(i == 0), stop=False)
                p.mm(psF[:], ones1, brow[0:1, h * 512:h * 512 + 512], start=False, stop=True)
                p.cp(s3[:, h * 512:(h + 1) * 512], psF[:])
            p.tt(s4[:], s1[:], s3[:], ALU.add)
            p.dma(x3[c * 128:(c + 1) * 128, :], s4[:])
        for i in range(8):
            p.dma(wg[:, i * 8:(i + 1) * 8], wgT[i * 128:(i + 1) * 128, :])
        for g in range(TOK // QT):
            t0 = g * QT
            for c in range(nch):
                r0 = t0 + c * 128
                p.dma(s1[:], x3[r0:r0 + 128, :])
                emit_ln(p, s1, s2, s3, s4, mrow, mrow)
                emit_tp8(p, s4, lnTg[:, c * 1024:(c + 1) * 1024], ps_t, identt[:])
                for i in range(8):
                    p.mm(psG[:, 0:8], lnTg[:, c * 1024 + i * 128:c * 1024 + i * 128 + 128],
                         wg[:, i * 8:(i + 1) * 8], start=(i == 0), stop=(i == 7))
                emit_router(p, psG, pscr, comb[:, c * 8:(c + 1) * 8], mrow)
                p.memset(accg[:, c * 1024:(c + 1) * 1024], 0.0)
            for e in range(8):
                p.dma(b2r[0:1, :], b2[e:e + 1, :])
                for hq in range(4):
                    p.dma(b1r[0:1, :], b1[e:e + 1, hq * 1024:(hq + 1) * 1024])
                    for i in range(8):
                        p.dma(w1s[:, i * 1024:(i + 1) * 1024],
                              W1[e, i * 128:(i + 1) * 128, hq * 1024:(hq + 1) * 1024])
                    w2q = W2[e, hq * 1024:(hq + 1) * 1024, :]
                    for j in range(8):
                        p.dma(w2s[:, j * 1024:(j + 1) * 1024],
                              w2q[j * 128:(j + 1) * 128, :])
                    for c in range(nch):
                        lt = lnTg[:, c * 1024:(c + 1) * 1024]
                        for h in range(2):
                            ps = psA if h == 0 else psB
                            for i in range(8):
                                p.mm(ps[:], lt[:, i * 128:(i + 1) * 128],
                                     w1s[:, i * 1024 + h * 512:i * 1024 + h * 512 + 512],
                                     start=(i == 0), stop=False)
                            p.mm(ps[:], ones1,
                                 b1r[0:1, h * 512:h * 512 + 512],
                                 start=False, stop=True)
                            p.act(s1[:, h * 512:(h + 1) * 512], ps[:], AF.Gelu)
                        emit_tp8(p, s1, s2, ps_t, identt[:])
                        for h in range(2):
                            ps = psA if h == 0 else psB
                            for j in range(8):
                                p.mm(ps[:], s2[:, j * 128:(j + 1) * 128],
                                     w2s[:, j * 1024 + h * 512:j * 1024 + h * 512 + 512],
                                     start=(j == 0), stop=(hq == 3))
                            y = yac[:, c * 1024 + h * 512:c * 1024 + h * 512 + 512]
                            if hq == 0:
                                p.cp(y, ps[:])
                            elif hq < 3:
                                p.tt(y, y, ps[:], ALU.add)
                            else:
                                p.tt(y, y, ps[:], ALU.add)
                                a = accg[:, c * 1024 + h * 512:c * 1024 + h * 512 + 512]
                                p.mm(ps[:], ones1,
                                     b2r[0:1, h * 512:h * 512 + 512], start=True, stop=True)
                                p.tt(y, y, ps[:], ALU.add)
                                p.stt(a, y, comb[:, c * 8 + e:c * 8 + e + 1], a,
                                      ALU.mult, ALU.add)
            for c in range(nch):
                r0 = t0 + c * 128
                p.dma(s1[:], x3[r0:r0 + 128, :])
                p.tt(s2[:], s1[:], accg[:, c * 1024:(c + 1) * 1024], ALU.add)
                p.dma(x4o[r0:r0 + 128, :], s2[:])
        p.build_block(nc)
    return nc


def build_D():
    nc = bass.Bass("TRN2", target_bir_lowering=False)
    p = Prog()
    TOK = 2048
    QT = 512
    FH = 4096
    x3 = nc.dram_tensor("x3", [TOK, D], F32, kind="ExternalInput").ap()
    wgT = nc.dram_tensor("wgT", [D, 8], F32, kind="ExternalInput").ap()
    W1 = nc.dram_tensor("W1", [8, D, FH], F32, kind="ExternalInput").ap()
    b1 = nc.dram_tensor("b1", [8, FH], F32, kind="ExternalInput").ap()
    W2 = nc.dram_tensor("W2", [8, FH, D], F32, kind="ExternalInput").ap()
    b2 = nc.dram_tensor("b2", [8, D], F32, kind="ExternalInput").ap()
    cns = nc.dram_tensor("cns", [128, 128], F32, kind="ExternalInput").ap()
    rws = nc.dram_tensor("rws", [1, 128], F32, kind="ExternalInput").ap()
    x4o = nc.dram_tensor("x4o", [TOK, D], F32, kind="ExternalOutput").ap()

    nch = QT // 128
    st = ExitStack()
    with st:
        EN = st.enter_context
        identt = EN(nc.sbuf_tensor("ident", [128, 128], F32))
        rows = EN(nc.sbuf_tensor("rows", [1, 128], F32))
        wg = EN(nc.sbuf_tensor("wg", [128, 64], F32))
        w1s = EN(nc.sbuf_tensor("w1s", [128, 8192], F32))
        w2s = EN(nc.sbuf_tensor("w2s", [128, 8192], F32))
        b1r = EN(nc.sbuf_tensor("b1r", [1, FH], F32))
        b2r = EN(nc.sbuf_tensor("b2r", [1, D], F32))
        lnTg = EN(nc.sbuf_tensor("lnTg", [128, QT * 8], F32))
        accg = EN(nc.sbuf_tensor("accg", [128, QT * 8], F32))
        yac = EN(nc.sbuf_tensor("yac", [128, QT * 8], F32))
        s1 = EN(nc.sbuf_tensor("s1", [128, D], F32))
        s2 = EN(nc.sbuf_tensor("s2", [128, D], F32))
        s3 = EN(nc.sbuf_tensor("s3", [128, D], F32))
        s4 = EN(nc.sbuf_tensor("s4", [128, D], F32))
        mrow = EN(nc.sbuf_tensor("mrow", [128, 8], F32))
        pscr = EN(nc.sbuf_tensor("pscr", [128, 40], F32))
        comb = EN(nc.sbuf_tensor("comb", [128, 8 * (QT // 128)], F32))
        ps_t = EN(nc.psum_tensor("ps_t", [128, 128], F32))
        psA = EN(nc.psum_tensor("psA", [128, 512], F32))
        psB = EN(nc.psum_tensor("psB", [128, 512], F32))
        psG = EN(nc.psum_tensor("psG", [128, 128], F32))
        ones1 = rows[0:1, 0:128]
        p.dma(identt[:], cns)
        p.dma(rows[:], rws)
        for i in range(8):
            p.dma(wg[:, i * 8:(i + 1) * 8], wgT[i * 128:(i + 1) * 128, :])
        for g in range(TOK // QT):
            t0 = g * QT
            for c in range(nch):
                r0 = t0 + c * 128
                p.dma(s1[:], x3[r0:r0 + 128, :])
                emit_ln(p, s1, s2, s3, s4, mrow, mrow)
                emit_tp8(p, s4, lnTg[:, c * 1024:(c + 1) * 1024], ps_t, identt[:])
                for i in range(8):
                    p.mm(psG[:, 0:8], lnTg[:, c * 1024 + i * 128:c * 1024 + i * 128 + 128],
                         wg[:, i * 8:(i + 1) * 8], start=(i == 0), stop=(i == 7))
                emit_router(p, psG, pscr, comb[:, c * 8:(c + 1) * 8], mrow)
                p.memset(accg[:, c * 1024:(c + 1) * 1024], 0.0)
            for e in range(8):
                p.dma(b1r[0:1, :], b1[e:e + 1, :])
                p.dma(b2r[0:1, :], b2[e:e + 1, :])
                for hq in range(4):
                    for i in range(8):
                        p.dma(w1s[:, i * 1024:(i + 1) * 1024],
                              W1[e, i * 128:(i + 1) * 128, hq * 1024:(hq + 1) * 1024])
                    w2q = W2[e, hq * 1024:(hq + 1) * 1024, :]
                    for j in range(8):
                        p.dma(w2s[:, j * 1024:(j + 1) * 1024],
                              w2q[j * 128:(j + 1) * 128, :])
                    for c in range(nch):
                        lt = lnTg[:, c * 1024:(c + 1) * 1024]
                        for h in range(2):
                            ps = psA if h == 0 else psB
                            for i in range(8):
                                p.mm(ps[:], lt[:, i * 128:(i + 1) * 128],
                                     w1s[:, i * 1024 + h * 512:i * 1024 + h * 512 + 512],
                                     start=(i == 0), stop=False)
                            p.mm(ps[:], ones1,
                                 b1r[0:1, hq * 1024 + h * 512:hq * 1024 + h * 512 + 512],
                                 start=False, stop=True)
                            p.act(s1[:, h * 512:(h + 1) * 512], ps[:], AF.Gelu)
                        emit_tp8(p, s1, s2, ps_t, identt[:])
                        for h in range(2):
                            ps = psA if h == 0 else psB
                            for j in range(8):
                                p.mm(ps[:], s2[:, j * 128:(j + 1) * 128],
                                     w2s[:, j * 1024 + h * 512:j * 1024 + h * 512 + 512],
                                     start=(j == 0), stop=(hq == 3))
                            y = yac[:, c * 1024 + h * 512:c * 1024 + h * 512 + 512]
                            if hq == 0:
                                p.cp(y, ps[:])
                            elif hq < 3:
                                p.tt(y, y, ps[:], ALU.add)
                            else:
                                p.tt(y, y, ps[:], ALU.add)
                                a = accg[:, c * 1024 + h * 512:c * 1024 + h * 512 + 512]
                                p.mm(ps[:], ones1,
                                     b2r[0:1, h * 512:h * 512 + 512], start=True, stop=True)
                                p.tt(y, y, ps[:], ALU.add)
                                p.stt(a, y, comb[:, c * 8 + e:c * 8 + e + 1], a,
                                      ALU.mult, ALU.add)
            for c in range(nch):
                r0 = t0 + c * 128
                p.dma(s1[:], x3[r0:r0 + 128, :])
                p.tt(s2[:], s1[:], accg[:, c * 1024:(c + 1) * 1024], ALU.add)
                p.dma(x4o[r0:r0 + 128, :], s2[:])
        p.build_block(nc)
    return nc


_DBG = {}


def _run(nc, in_maps, names):
    res = run_bass_kernel_spmd(nc, in_maps, core_ids=list(range(NCORE)))
    return [[np.asarray(r[n]) for n in names] for r in res.results]


def kernel(**inp):
    f = np.float32
    x = np.ascontiguousarray(inp["L0_attn_x"], dtype=f)
    ident = np.eye(128, dtype=f)
    cumA = np.triu(np.ones((128, 128), f))
    shiftA = np.eye(128, k=1, dtype=f)
    rowsA = np.zeros((1, 256), f)
    rowsA[0, 0:128] = 1.0
    rowsA[0, 128] = 1.0
    rows1 = np.ones((1, 128), f)
    cnsA = np.concatenate([ident, cumA, shiftA], axis=1)

    wqT = np.ascontiguousarray(inp["L0_attn_Wq1"].T, f)
    wkT = np.ascontiguousarray(inp["L0_attn_Wk1"].T, f)
    wk2T = np.ascontiguousarray(inp["L0_attn_Wk2"].T, f)

    ncA = build_A()
    maps = []
    for c in range(NCORE):
        b, g = c // 4, c % 4
        sel = g * 256
        maps.append({
            "xb": np.ascontiguousarray(np.roll(x[b], -sel, axis=1)),
            "wq": np.ascontiguousarray(np.roll(wqT[:, sel:sel + 256], -sel, axis=0)),
            "wk": np.ascontiguousarray(np.roll(wkT[:, sel:sel + 256], -sel, axis=0)),
            "wk2": np.ascontiguousarray(np.roll(wk2T[:, sel:sel + 256], -sel, axis=0)),
            "bq": np.ascontiguousarray(inp["L0_attn_bq1"][sel:sel + 256], f),
            "bk": np.ascontiguousarray(inp["L0_attn_bk1"][sel:sel + 256], f),
            "bk2": np.ascontiguousarray(inp["L0_attn_bk2"][sel:sel + 256], f),
            "cns": cnsA, "rws": rowsA,
        })
    outA = _run(ncA, maps, ["aout"])
    A = np.zeros((BATCH, SEQ, D), f)
    for c in range(NCORE):
        b, g = c // 4, c % 4
        A[b][:, g * 256:(g + 1) * 256] = outA[c][0]
    _DBG["A"] = A

    xf = x.reshape(NTOK, D)
    Af = A.reshape(NTOK, D)
    ncB = build_B()
    maps = []
    shared = {
        "wprojT": np.ascontiguousarray(inp["L0_attn_Wproj"].T, f),
        "bproj": np.ascontiguousarray(inp["L0_attn_bproj"], f),
        "routw": np.ascontiguousarray(inp["L0_ffn_router_w"], f),
        "w1": np.ascontiguousarray(inp["L0_ffn_w1"], f),
        "w2": np.ascontiguousarray(inp["L0_ffn_w2"], f),
        "sgu": np.ascontiguousarray(inp["L0_ffn_shared_gate_up"], f),
        "sdn": np.ascontiguousarray(inp["L0_ffn_shared_down"], f),
        "sgw": np.ascontiguousarray(inp["L0_ffn_shared_gate_w"], f),
        "cns": ident, "rws": rows1,
    }
    for c in range(NCORE):
        m = dict(shared)
        m["xsh"] = np.ascontiguousarray(xf[c * 2048:(c + 1) * 2048])
        m["ash"] = np.ascontiguousarray(Af[c * 2048:(c + 1) * 2048])
        maps.append(m)
    outB = _run(ncB, maps, ["x2o"])
    x2 = np.concatenate([o[0] for o in outB], axis=0)
    _DBG["x2"] = x2

    x2r = x2.reshape(BATCH, SEQ, D)
    ncC = build_CD()
    maps = []
    sharedC = {
        "wqT": np.ascontiguousarray(inp["L1_attn_Wq"].T, f),
        "wkT": np.ascontiguousarray(inp["L1_attn_Wk"].T, f),
        "wvT": np.ascontiguousarray(inp["L1_attn_Wv"].T, f),
        "wfT": np.ascontiguousarray(inp["L1_attn_Wf"].T, f),
        "bf": np.ascontiguousarray(inp["L1_attn_bf"], f),
        "cns": ident, "rws": rows1,
        "wgT": np.ascontiguousarray(inp["L1_ffn_w_gate"].T, f),
        "W1": np.ascontiguousarray(inp["L1_ffn_W1"], f),
        "b1": np.ascontiguousarray(inp["L1_ffn_b1"], f),
        "W2": np.ascontiguousarray(inp["L1_ffn_W2"], f),
        "b2": np.ascontiguousarray(inp["L1_ffn_b2"], f),
    }
    for c in range(NCORE):
        b, gq = c // 4, c % 4
        qbase = gq * 2048
        ext = np.zeros((2560, D), f)
        lo = qbase - 256
        hi = qbase + 2048 + 256
        slo, shi = max(lo, 0), min(hi, SEQ)
        ext[slo - lo:shi - lo] = x2r[b, slo:shi]
        mask = np.full((16, 128, 640), -1e30, f)
        for cc in range(16):
            qpos = qbase + cc * 128 + np.arange(128)
            spos = lo + cc * 128 + np.arange(640)
            ok = (np.abs(qpos[:, None] - spos[None, :]) <= 256) & \
                 (spos[None, :] >= 0) & (spos[None, :] < SEQ)
            mask[cc][ok] = 0.0
        m = dict(sharedC)
        m["xe"] = ext
        m["msk"] = mask
        maps.append(m)
    outC = _run(ncC, maps, ["x4o"])
    x4 = np.concatenate([o[0] for o in outC], axis=0)
    return x4.reshape(BATCH, SEQ, D)
